# version 01
# baseline (speedup 1.0000x reference)
import numpy as np
import ml_dtypes
import concourse.bass as bass
import concourse.mybir as mybir
from concourse.bass_utils import run_bass_kernel_spmd
from concourse.alu_op_type import AluOpType as OP
from contextlib import ExitStack

F32 = mybir.dt.float32
BF16 = mybir.dt.bfloat16
AF = mybir.ActivationFunctionType
AXX = mybir.AxisListType.X

NCORE = 8
TOK = 2048
NT = 4
T = 512
TBN = 4
D = 1024
KC = 8
E = 8
F = 2048
FC = 16
FH = 8
BIG = 1.0e9
EPS = 1e-5

ENGINES = ("sync", "gpsimd", "scalar", "vector", "tensor")
SEMS = ("sW", "sG", "sO", "sT", "sV", "sS", "sX")

FFN_ON = True
STAGE = 99


class Em:

    def __init__(self):
        self.progs = {e: [] for e in ENGINES}
        self.counts = {s: 0 for s in SEMS}
        self.last_wait = {e: {} for e in ENGINES}

    def op(self, eng, fn, waits=(), inc=None, inc_n=1):
        w = []
        lw = self.last_wait[eng]
        for sv in waits:
            if sv is None:
                continue
            s, v = sv
            assert v <= self.counts[s], f"wait on future {s}>={v} (now {self.counts[s]})"
            if lw.get(s, 0) < v:
                lw[s] = v
                w.append((s, v))
        mark = None
        if inc is not None:
            self.counts[inc] += inc_n
            mark = (inc, self.counts[inc])
        self.progs[eng].append((w, fn, mark, inc_n))
        return mark


def later(*ms):
    best = None
    for m in ms:
        if m is None:
            continue
        if best is None:
            best = m
        else:
            assert m[0] == best[0], (m, best)
            if m[1] > best[1]:
                best = m
    return best


def build():
    nc = bass.Bass("TRN2", target_bir_lowering=False, detect_race_conditions=False)

    xfm_h = nc.dram_tensor("x_fm", [D, TOK], F32, kind="ExternalInput")
    xtm_h = nc.dram_tensor("x_tm", [TOK, D], F32, kind="ExternalInput")
    w_h = [nc.dram_tensor(n, [D, D], F32, kind="ExternalInput")
           for n in ("wqT", "wkT", "wgT", "wvT", "woT")]
    rt_h = nc.dram_tensor("router", [D, E], F32, kind="ExternalInput")
    kg_h = nc.dram_tensor("kg", [E, D, F], F32, kind="ExternalInput")
    ku_h = nc.dram_tensor("ku", [E, D, F], F32, kind="ExternalInput")
    kd_h = nc.dram_tensor("kd", [E, F, D], F32, kind="ExternalInput")
    ld_h = nc.dram_tensor("log_decay", [D], F32, kind="ExternalInput")
    idn_h = nc.dram_tensor("ident", [128, 128], F32, kind="ExternalInput")
    dd_h = nc.dram_tensor("dmat", [128, 128], F32, kind="ExternalInput")
    bb_h = nc.dram_tensor("bmat", [128, 128], F32, kind="ExternalInput")
    out_h = nc.dram_tensor("out", [TOK, D], F32, kind="ExternalOutput")
    dbh_h = nc.dram_tensor("dbg_hi", [128, 8192], BF16, kind="ExternalOutput")
    dbl_h = nc.dram_tensor("dbg_lo", [128, 8192], BF16, kind="ExternalOutput")
    dbf_h = nc.dram_tensor("dbg_f32", [128, 4096], F32, kind="ExternalOutput")
    dbs_h = nc.dram_tensor("dbg_stt", [128, 64], F32, kind="ExternalOutput")

    swa_hi = nc.dram_tensor("swa_hi", [5, 128, KC * D], BF16)
    swa_lo = nc.dram_tensor("swa_lo", [5, 128, KC * D], BF16)
    skg_hi = nc.dram_tensor("skg_hi", [E, 128, KC * F], BF16)
    skg_lo = nc.dram_tensor("skg_lo", [E, 128, KC * F], BF16)
    sku_hi = nc.dram_tensor("sku_hi", [E, 128, KC * F], BF16)
    sku_lo = nc.dram_tensor("sku_lo", [E, 128, KC * F], BF16)
    skd_hi = nc.dram_tensor("skd_hi", [E, 128, FC * D], BF16)
    skd_lo = nc.dram_tensor("skd_lo", [E, 128, FC * D], BF16)

    with ExitStack() as st:
        def sb(n, shape, dt):
            return st.enter_context(nc.sbuf_tensor(n, shape, dt))

        A32 = sb("A32", [128, 4096], F32)
        XT32 = sb("XT32", [128, 4096], F32)
        X1TM = sb("X1TM", [128, 4096], F32)
        GA32 = sb("GA32", [128, 1024], F32)
        FU32 = sb("FU32", [128, 1024], F32)
        RTI = sb("RTI", [128, 160], F32)
        STT = sb("STT", [128, 64], F32)
        W32 = sb("W32", [128, 32], F32)
        LDC = sb("LDC", [128, 4], F32)
        CIDN = sb("CIDN", [128, 128], F32)
        CD = sb("CD", [128, 128], F32)
        CB = sb("CB", [128, 128], F32)
        CMSK = sb("CMSK", [128, 128], F32)
        RT_HI = sb("RT_HI", [128, 64], BF16)
        RT_LO = sb("RT_LO", [128, 64], BF16)
        SB32 = sb("SB32", [128, 128], F32)
        SBH = sb("SBH", [128, 128], BF16)
        SBL = sb("SBL", [128, 128], BF16)
        QH = sb("QH", [128, 4096], BF16)
        QL = sb("QL", [128, 4096], BF16)
        KH = sb("KH", [128, 4096], BF16)
        KL = sb("KL", [128, 4096], BF16)
        VH = sb("VH", [128, 4096], BF16)
        VL = sb("VL", [128, 4096], BF16)
        FUH = sb("FUH", [128, 8192], BF16)
        FUL = sb("FUL", [128, 8192], BF16)
        AWH = sb("AWH", [128, 2048], BF16)
        AWL = sb("AWL", [128, 2048], BF16)
        KGH = sb("KGH", [128, 3072], BF16)
        KGL = sb("KGL", [128, 3072], BF16)
        KUH = sb("KUH", [128, 3072], BF16)
        KUL = sb("KUL", [128, 3072], BF16)
        KDH = sb("KDH", [128, 3072], BF16)
        KDL = sb("KDL", [128, 3072], BF16)
        P0F = sb("P0F", [128, 2048], F32)
        P0H = sb("P0H", [128, 2048], BF16)
        P0L = sb("P0L", [128, 2048], BF16)

        PS = [st.enter_context(nc.psum_tensor(f"ps{i}", [128, 512], F32)) for i in range(8)]
        semh = {s: st.enter_context(nc.semaphore(s)) for s in SEMS}
        block = st.enter_context(nc.Block())

        em = Em()
        bankg = [[] for _ in range(8)]

        def mm(out, lh, rh, start, stop, waits=(), inc=None):
            return em.op(
                "tensor",
                (lambda t, o=out, l=lh, r=rh, s=start, p=stop:
                 t.matmul(o, l, r, start=s, stop=p)),
                waits=waits, inc=inc)

        def mm3(out, lh, ll, rh, rl, first, last, waits=(), inc=None):
            mm(out, lh, rh, first, False, waits=waits)
            mm(out, lh, rl, False, False)
            return mm(out, ll, rh, False, last, inc=inc)

        def vop(fn, waits=(), inc="sV"):
            return em.op("vector", fn, waits=waits, inc=inc)

        def sop(fn, waits=(), inc="sS"):
            return em.op("scalar", fn, waits=waits, inc=inc)

        def gdma(fn, waits=(), inc="sG"):
            return em.op("gpsimd", fn, waits=waits, inc=inc, inc_n=16)

        def wdma(fn, waits=()):
            return em.op("sync", fn, waits=waits, inc="sW", inc_n=16)

        def sslice(t, c0, n):
            return t[:, c0:c0 + n]

        c_idn = gdma(lambda g: g.dma_start(out=CIDN[:], in_=idn_h.ap()))
        c_dd = gdma(lambda g: g.dma_start(out=CD[:], in_=dd_h.ap()))
        c_bb = gdma(lambda g: g.dma_start(out=CB[:], in_=bb_h.ap()))
        c_ld = gdma(lambda g: g.dma_start(out=LDC[:, 0:1],
                                          in_=bass.AP(ld_h, 0, [[0, 128], [1, 1]])))
        for kc in range(KC):
            c_rt = gdma(lambda g, k=kc: g.dma_start(
                out=bass.AP(P0F, k * E, [[2048, 128], [1, E]]),
                in_=bass.AP(rt_h, k * 128 * E, [[E, 128], [1, E]])))
        m = sop(lambda s: s.copy(RT_HI[:], P0F[:, 0:64]), waits=[c_rt])
        rt_ready_s = m
        rt_ready_v = vop(lambda v: v.tensor_tensor(RT_LO[:], P0F[:, 0:64], RT_HI[:],
                                                   op=OP.subtract), waits=[c_rt, m])
        m = sop(lambda s: s.activation(LDC[:, 1:2], LDC[:, 0:1], AF.Sigmoid), waits=[c_ld])
        m = sop(lambda s: s.activation(LDC[:, 2:3], LDC[:, 1:2], AF.Ln), waits=[m])
        m = sop(lambda s: s.activation(CMSK[:], CD[:], AF.Exp, scale=LDC[:, 2:3]),
                waits=[c_dd, m])
        msk_ready = vop(lambda v: v.tensor_tensor(CMSK[:], CMSK[:], CB[:], op=OP.mult),
                        waits=[c_bb, m])
        c_eps = em.op("gpsimd", lambda g: g.memset(LDC[:, 3:4], EPS), inc="sX", inc_n=1)

        p0_slot_free = [[rt_ready_s, rt_ready_v], []]
        p0_idx = [0]
        scratch_ready = {}

        def phase0_unit(key, loads, out_hi_ap_fn, out_lo_ap_fn):
            sl = p0_idx[0] % 2
            p0_idx[0] += 1
            c0 = sl * 1024
            ld = None
            first = True
            for (a, co, wd) in loads:
                ld = gdma(lambda g, aa=a, c=c0 + co, ww=wd: g.dma_start(
                    out=bass.AP(P0F, c, [[2048, 128], [1, ww]]), in_=aa),
                    waits=(list(p0_slot_free[sl]) if first else ()))
                first = False
            hi = sop(lambda s, c=c0: s.copy(
                bass.AP(P0H, c, [[2048, 128], [1, 1024]]),
                bass.AP(P0F, c, [[2048, 128], [1, 1024]])), waits=[ld])
            lo = vop(lambda v, c=c0: v.tensor_tensor(
                bass.AP(P0L, c, [[2048, 128], [1, 1024]]),
                bass.AP(P0F, c, [[2048, 128], [1, 1024]]),
                bass.AP(P0H, c, [[2048, 128], [1, 1024]]), op=OP.subtract),
                waits=[ld, hi])
            gdma(lambda g, c=c0, f=out_hi_ap_fn: g.dma_start(
                out=f(), in_=bass.AP(P0H, c, [[2048, 128], [1, 1024]])), waits=[hi, lo])
            wr = gdma(lambda g, c=c0, f=out_lo_ap_fn: g.dma_start(
                out=f(), in_=bass.AP(P0L, c, [[2048, 128], [1, 1024]])))
            p0_slot_free[sl] = [wr]
            scratch_ready[key] = wr

        for w in range(5):
            if w == 3:
                for kc in range(KC):
                    loads = [(bass.AP(w_h[w], kc * 128 * D, [[D, 128], [1, D]]), 0, D)]
                    off = w * 128 * (KC * D) + kc * D
                    hi_fn = (lambda o=off: bass.AP(swa_hi, o, [[KC * D, 128], [1, D]]))
                    lo_fn = (lambda o=off: bass.AP(swa_lo, o, [[KC * D, 128], [1, D]]))
                    phase0_unit(("wa", w, kc), loads, hi_fn, lo_fn)
            else:
                for dc in range(KC):
                    loads = [(bass.AP(w_h[w], kc * 128 * D + dc * 128, [[D, 128], [1, 128]]),
                              kc * 128, 128) for kc in range(KC)]
                    off = w * 128 * (KC * D) + dc * 1024
                    hi_fn = (lambda o=off: bass.AP(swa_hi, o, [[KC * D, 128], [1, 1024]]))
                    lo_fn = (lambda o=off: bass.AP(swa_lo, o, [[KC * D, 128], [1, 1024]]))
                    phase0_unit(("wa", w, dc), loads, hi_fn, lo_fn)
            scratch_ready[("wa", w)] = scratch_ready[("wa", w, KC - 1)]

        if FFN_ON:
            for e in range(E):
                for mat, mh_, shi, slo in (("kg", kg_h, skg_hi, skg_lo),
                                           ("ku", ku_h, sku_hi, sku_lo)):
                    for fc in range(FC):
                        loads = [(bass.AP(mh_, e * D * F + kc * 128 * F + fc * 128,
                                          [[F, 128], [1, 128]]), kc * 128, 128)
                                 for kc in range(KC)]
                        off = e * 128 * (KC * F) + fc * 1024
                        hi_fn = (lambda o=off, t=shi: bass.AP(t, o, [[KC * F, 128], [1, 1024]]))
                        lo_fn = (lambda o=off, t=slo: bass.AP(t, o, [[KC * F, 128], [1, 1024]]))
                        phase0_unit((mat, e, fc), loads, hi_fn, lo_fn)
                for fc in range(FC):
                    loads = [(bass.AP(kd_h, e * F * D + fc * 128 * D, [[D, 128], [1, D]]), 0, D)]
                    off = e * 128 * (FC * D) + fc * 1024
                    hi_fn = (lambda o=off: bass.AP(skd_hi, o, [[FC * D, 128], [1, 1024]]))
                    lo_fn = (lambda o=off: bass.AP(skd_lo, o, [[FC * D, 128], [1, 1024]]))
                    phase0_unit(("kd", e, fc), loads, hi_fn, lo_fn)

        class Ring:
            def __init__(self, hi_sb, lo_sb, nslots, width):
                self.hi, self.lo, self.n, self.w = hi_sb, lo_sb, nslots, width
                self.lastread = [None] * nslots
                self.i = 0

            def load(self, key, src_hi, src_lo):
                sl = self.i % self.n
                self.i += 1
                c0 = sl * 1024
                wdma(lambda y, c=c0, a=src_hi, t=self.hi, W=self.w: y.dma_start(
                    out=bass.AP(t, c, [[W, 128], [1, 1024]]), in_=a),
                    waits=[self.lastread[sl], scratch_ready.get(key)])
                f = wdma(lambda y, c=c0, a=src_lo, t=self.lo, W=self.w: y.dma_start(
                    out=bass.AP(t, c, [[W, 128], [1, 1024]]), in_=a))
                return sl, f

            def consumed(self, sl, milestone):
                self.lastread[sl] = milestone

        ring_aw = Ring(AWH, AWL, 2, 2048)
        ring_kg = Ring(KGH, KGL, 3, 3072)
        ring_ku = Ring(KUH, KUL, 3, 3072)
        ring_kd = Ring(KDH, KDL, 3, 3072)

        ga_guard = {0: None, 1: None}
        fu_guard = {0: None, 1: None}

        g_A32_free = None
        g_XT_free = None
        g_X1_free = None
        g_QH_free = None
        g_KH_free = None
        g_VH_free = None
        g_FU_free = None
        out_ms = []

        for it in range(NT):
            tok0 = it * T
            ld_xfm = None
            for kc in range(KC):
                ld_xfm = gdma(lambda g, o=tok0, k=kc: g.dma_start(
                    out=bass.AP(A32, k * 512, [[4096, 128], [1, 512]]),
                    in_=bass.AP(xfm_h, k * 128 * TOK + o, [[TOK, 128], [1, 512]])),
                    waits=[g_A32_free])
            ld_xtm = None
            for tb in range(TBN):
                ld_xtm = gdma(lambda g, o=tok0 * D, t=tb: g.dma_start(
                    out=bass.AP(XT32, t * 1024, [[4096, 128], [1, 1024]]),
                    in_=bass.AP(xtm_h, (t * 128) * D + o, [[D, 128], [1, 1024]])),
                    waits=[g_XT_free])
            xs_hi = sop(lambda s: s.copy(sslice(FUH, 0, 4096), A32[:]),
                        waits=[ld_xfm, g_FU_free])
            xs_lo = vop(lambda v: v.tensor_tensor(sslice(FUL, 0, 4096), A32[:],
                                                  sslice(FUH, 0, 4096), op=OP.subtract),
                        waits=[ld_xfm, xs_hi, g_FU_free])


            def _stage_exit(last_v, hi_sb=None, lo_sb=None, f32_sb=None, n=4096):
                if it == 0:
                    if hi_sb is not None:
                        mo = gdma(lambda g, t=hi_sb, nn=n: g.dma_start(
                            out=bass.AP(dbh_h, 0, [[8192, 128], [1, nn]]),
                            in_=t[:, 0:nn]), waits=[last_v], inc="sO")
                        out_ms.append(mo)
                    if lo_sb is not None:
                        mo = gdma(lambda g, t=lo_sb, nn=n: g.dma_start(
                            out=bass.AP(dbl_h, 0, [[8192, 128], [1, nn]]),
                            in_=t[:, 0:nn]), waits=[last_v], inc="sO")
                        out_ms.append(mo)
                    if f32_sb is not None:
                        mo = gdma(lambda g, t=f32_sb: g.dma_start(
                            out=bass.AP(dbf_h, 0, [[4096, 128], [1, 4096]]),
                            in_=t[:, 0:4096]), waits=[last_v], inc="sO")
                        out_ms.append(mo)
                    mo = gdma(lambda g: g.dma_start(
                        out=dbs_h.ap(), in_=STT[:]), waits=[last_v], inc="sO")
                    out_ms.append(mo)
                mo = gdma(lambda g, o=tok0 * D: g.dma_start(
                    out=bass.AP(out_h, o, [[D, 128], [1, D]]),
                    in_=sslice(XT32, 0, 1024)), waits=[last_v], inc="sO")
                out_ms.append(mo)

            if STAGE <= 1:
                _stage_exit(xs_lo, FUH, FUL)
                continue

            evac = {}
            for which, wi, HI, LO in (("q", 0, QH, QL), ("k", 1, KH, KL), ("g", 2, None, None)):
                for dc in range(KC):
                    sl, fill = ring_aw.load(
                        ("wa", wi),
                        bass.AP(swa_hi, wi * 128 * KC * D + dc * 1024, [[KC * D, 128], [1, 1024]]),
                        bass.AP(swa_lo, wi * 128 * KC * D + dc * 1024, [[KC * D, 128], [1, 1024]]))
                    b = dc % 4
                    last = None
                    for kc in range(KC):
                        wl = ([fill, xs_hi, xs_lo] + bankg[b]) if kc == 0 else ()
                        last = mm3(PS[b][:, 0:512],
                                   sslice(AWH, sl * 1024 + kc * 128, 128),
                                   sslice(AWL, sl * 1024 + kc * 128, 128),
                                   sslice(FUH, kc * 512, 512),
                                   sslice(FUL, kc * 512, 512),
                                   kc == 0, kc == KC - 1,
                                   waits=wl, inc=("sT" if kc == KC - 1 else None))
                    ring_aw.consumed(sl, last)
                    if which == "g":
                        mg = sop(lambda s, d=dc, bb=b: s.activation(
                            sslice(A32, d * 512, 512), PS[bb][:, 0:512], AF.Sigmoid),
                            waits=[last, xs_lo])
                        bankg[b] = [mg]
                        evac[("g", dc)] = (mg, None)
                    else:
                        free = g_QH_free if which == "q" else g_KH_free
                        mh = sop(lambda s, d=dc, bb=b, t=HI: s.copy(
                            sslice(t, d * 512, 512), PS[bb][:, 0:512]),
                            waits=[last, free])
                        ml = vop(lambda v, d=dc, bb=b, th=HI, tl=LO: v.tensor_tensor(
                            sslice(tl, d * 512, 512), PS[bb][:, 0:512],
                            sslice(th, d * 512, 512), op=OP.subtract),
                            waits=[last, mh, free])
                        bankg[b] = [mh, ml]
                        evac[(which, dc)] = (mh, ml)

            if STAGE <= 2:
                _stage_exit(later(*[evac[("q", dc)][1] for dc in range(KC)]), QH, QL)
                continue

            v_last = {}
            v_mm_last = None
            for kc in range(KC):
                sl, fill = ring_aw.load(
                    ("wa", 3),
                    bass.AP(swa_hi, 3 * 128 * KC * D + kc * 1024, [[KC * D, 128], [1, 1024]]),
                    bass.AP(swa_lo, 3 * 128 * KC * D + kc * 1024, [[KC * D, 128], [1, 1024]]))
                last = None
                for nh in range(2):
                    for tb in range(TBN):
                        b = nh * 4 + tb
                        wl = [fill] + (bankg[b] if kc == 0 else [])
                        want_inc = (kc == KC - 1) or (nh == 1 and tb == TBN - 1)
                        last = mm3(PS[b][:, 0:512],
                                   sslice(FUH, kc * 512 + tb * 128, 128),
                                   sslice(FUL, kc * 512 + tb * 128, 128),
                                   sslice(AWH, sl * 1024 + nh * 512, 512),
                                   sslice(AWL, sl * 1024 + nh * 512, 512),
                                   kc == 0, kc == KC - 1,
                                   waits=wl, inc=("sT" if want_inc else None))
                        if kc == KC - 1:
                            v_last[(nh, tb)] = last
                ring_aw.consumed(sl, last)
                v_mm_last = last
            v_ev = {}
            for nh in range(2):
                for tb in range(TBN):
                    b = nh * 4 + tb
                    mh = sop(lambda s, t=tb, n=nh, bb=b: s.copy(
                        sslice(VH, t * 1024 + n * 512, 512), PS[bb][:, 0:512]),
                        waits=[v_last[(nh, tb)], g_VH_free])
                    ml = vop(lambda v, t=tb, n=nh, bb=b: v.tensor_tensor(
                        sslice(VL, t * 1024 + n * 512, 512), PS[bb][:, 0:512],
                        sslice(VH, t * 1024 + n * 512, 512), op=OP.subtract),
                        waits=[v_last[(nh, tb)], mh, g_VH_free])
                    bankg[b] = [mh, ml]
                    v_ev[(nh, tb)] = (mh, ml)

            if STAGE <= 3:
                _stage_exit(later(*[v_ev[k][1] for k in v_ev]), VH, VL)
                continue

            qk_ready_s = later(*([evac[("q", dc)][0] for dc in range(KC)] +
                                 [evac[("k", dc)][0] for dc in range(KC)]))
            qk_ready_v = later(*([evac[("q", dc)][1] for dc in range(KC)] +
                                 [evac[("k", dc)][1] for dc in range(KC)]))
            v_ready_s = later(*[v_ev[k][0] for k in v_ev])
            v_ready_v = later(*[v_ev[k][1] for k in v_ev])
            gate_ready = later(*[evac[("g", dc)][0] for dc in range(KC)])

            s_done = {}
            sb_hi = {}
            sb_lo = {}
            o_done = {}
            oev = []

            def emit_S(g):
                b = g % 2
                last = None
                for dc in range(KC):
                    wl = ([qk_ready_s, qk_ready_v, msk_ready] + bankg[b]) if dc == 0 else ()
                    last = mm3(PS[b][:, 0:128],
                               sslice(KH, dc * 512 + g * 128, 128),
                               sslice(KL, dc * 512 + g * 128, 128),
                               sslice(QH, dc * 512 + g * 128, 128),
                               sslice(QL, dc * 512 + g * 128, 128),
                               dc == 0, dc == KC - 1,
                               waits=wl, inc=("sT" if dc == KC - 1 else None))
                s_done[g] = (last, b)

            def emit_S_evac(g, prev_o_last):
                last, b = s_done[g]
                wl = [last]
                if g - 1 in sb_hi:
                    wl.append(sb_hi[g - 1])
                mt = vop(lambda v, bb=b: v.tensor_tensor(SB32[:], PS[bb][:, 0:128],
                                                         CMSK[:], op=OP.mult), waits=wl)
                bankg[b] = [mt]
                mh = sop(lambda s: s.copy(SBH[:], SB32[:]), waits=[mt, prev_o_last])
                ml = vop(lambda v: v.tensor_tensor(SBL[:], SB32[:], SBH[:], op=OP.subtract),
                         waits=[mh, prev_o_last])
                sb_hi[g] = mh
                sb_lo[g] = ml
                return mh, ml

            def emit_O(g, mh, ml):
                for dc in range(KC):
                    b = 2 + (dc % 4)
                    wl = [mh, ml, v_ready_s, v_ready_v] + bankg[b]
                    last = mm3(PS[b][:, 0:128],
                               sslice(VH, g * 1024 + dc * 128, 128),
                               sslice(VL, g * 1024 + dc * 128, 128),
                               SBH[:], SBL[:],
                               True, True, waits=wl, inc="sT")
                    o_done[(g, dc)] = last
                    me = vop(lambda v, d=dc, gg=g, bb=b: v.tensor_tensor(
                        sslice(X1TM, d * 512 + gg * 128, 128), PS[bb][:, 0:128],
                        sslice(A32, d * 512 + gg * 128, 128), op=OP.mult),
                        waits=[last, gate_ready, g_X1_free])
                    bankg[b] = [me]
                    oev.append(me)

            emit_S(0)
            emit_S(1)
            sev = emit_S_evac(0, None)
            emit_O(0, *sev)
            emit_S(2)
            sev = emit_S_evac(1, o_done[(0, KC - 1)])
            emit_O(1, *sev)
            emit_S(3)
            sev = emit_S_evac(2, o_done[(1, KC - 1)])
            emit_O(2, *sev)
            sev = emit_S_evac(3, o_done[(2, KC - 1)])
            emit_O(3, *sev)

            y_ready = later(*oev)
            s_last_mm = s_done[3][0]
            y_hi = sop(lambda s: s.copy(QH[:], X1TM[:]),
                       waits=[y_ready, s_last_mm, g_QH_free])
            y_lo = vop(lambda v: v.tensor_tensor(QL[:], X1TM[:], QH[:], op=OP.subtract),
                       waits=[y_hi, s_last_mm, g_QH_free])

            if STAGE <= 4:
                _stage_exit(y_lo, QH, QL)
                continue

            z_ev = []
            z_mm_last = None
            for dc in range(KC):
                sl, fill = ring_aw.load(
                    ("wa", 4),
                    bass.AP(swa_hi, 4 * 128 * KC * D + dc * 1024, [[KC * D, 128], [1, 1024]]),
                    bass.AP(swa_lo, 4 * 128 * KC * D + dc * 1024, [[KC * D, 128], [1, 1024]]))
                b = dc % 4
                last = None
                for kc in range(KC):
                    wl = ([fill, y_hi, y_lo] + bankg[b]) if kc == 0 else ()
                    last = mm3(PS[b][:, 0:512],
                               sslice(AWH, sl * 1024 + kc * 128, 128),
                               sslice(AWL, sl * 1024 + kc * 128, 128),
                               sslice(QH, kc * 512, 512),
                               sslice(QL, kc * 512, 512),
                               kc == 0, kc == KC - 1,
                               waits=wl, inc=("sT" if kc == KC - 1 else None))
                ring_aw.consumed(sl, last)
                z_mm_last = last
                mz = vop(lambda v, d=dc, bb=b: v.tensor_copy(sslice(A32, d * 512, 512),
                                                             PS[bb][:, 0:512]),
                         waits=[last])
                bankg[b] = [mz]
                z_ev.append(mz)
            z_ready = later(*z_ev)
            if STAGE <= 4.2:
                _stage_exit(z_ready, f32_sb=A32)
                continue

            tp_done = {}
            for tb in range(TBN):
                for dcg in range(2):
                    b = tb * 2 + dcg
                    last = None
                    for j in range(4):
                        dc = dcg * 4 + j
                        wl = ([z_ready] + bankg[b]) if j == 0 else ()
                        last = em.op("tensor",
                                     (lambda t, bb=b, jj=j, d=dc, tt=tb:
                                      t.transpose(PS[bb][:, jj * 128:jj * 128 + 128],
                                                  sslice(A32, d * 512 + tt * 128, 128),
                                                  CIDN[:])),
                                     waits=wl, inc=("sT" if j == 3 else None))
                    tp_done[(tb, dcg)] = last
            tp_all = tp_done[(TBN - 1, 1)]
            h_st = {}
            for tb in range(TBN):
                for dcg in range(2):
                    b = tb * 2 + dcg
                    ms = vop(lambda v, t=tb, g2=dcg, bb=b: v.scalar_tensor_tensor(
                        sslice(A32, t * 1024 + g2 * 512, 512),
                        sslice(XT32, t * 1024 + g2 * 512, 512),
                        2.0, PS[bb][:, 0:512], op0=OP.mult, op1=OP.add),
                        waits=[tp_all, ld_xtm])
                    bankg[b] = [ms]
                    h_st[(tb, dcg)] = ms

            if STAGE <= 4.4:
                _stage_exit(later(*h_st.values()), f32_sb=A32)
                continue

            ln1_x1 = {}
            for tb in range(TBN):
                c = tb * 1024
                m0 = vop(lambda v, cc=c, t=tb: v.reduce_sum(STT[:, t:t + 1],
                                                            sslice(A32, cc, 1024), axis=AXX),
                         waits=[h_st[(tb, 0)], h_st[(tb, 1)]])
                m1 = vop(lambda v, t=tb: v.tensor_scalar(STT[:, t:t + 1], STT[:, t:t + 1],
                                                         1.0 / D, None, op0=OP.mult),
                         waits=[m0])
                mhm = vop(lambda v, cc=c, t=tb: v.tensor_scalar(
                    sslice(X1TM, cc, 1024), sslice(A32, cc, 1024),
                    STT[:, t:t + 1], None, op0=OP.subtract), waits=[m1])
                msqa = sop(lambda s, cc=c, t=tb: s.activation(
                    sslice(XT32, cc, 1024), sslice(X1TM, cc, 1024), AF.Square),
                    waits=[mhm])
                m2 = vop(lambda v, cc=c, t=tb: v.reduce_sum(STT[:, 8 + t:9 + t],
                                                            sslice(XT32, cc, 1024), axis=AXX),
                         waits=[msqa])
                mv = vop(lambda v, t=tb: v.tensor_scalar(STT[:, 8 + t:9 + t],
                                                         STT[:, 8 + t:9 + t],
                                                         1.0 / D, None, op0=OP.mult),
                         waits=[m2])
                msq = sop(lambda s, t=tb: s.activation(STT[:, 16 + t:17 + t],
                                                       STT[:, 8 + t:9 + t],
                                                       AF.Sqrt, bias=LDC[:, 3:4]),
                          waits=[mv, c_eps])
                mr = vop(lambda v, t=tb: v.reciprocal(STT[:, 24 + t:25 + t],
                                                      STT[:, 16 + t:17 + t]),
                         waits=[msq])
                mx1 = vop(lambda v, cc=c, t=tb: v.tensor_scalar(
                    sslice(X1TM, cc, 1024), sslice(X1TM, cc, 1024),
                    STT[:, 24 + t:25 + t], None, op0=OP.mult), waits=[mr])
                ln1_x1[tb] = mx1

            if STAGE <= 4.6:
                _stage_exit(later(*ln1_x1.values()), f32_sb=X1TM)
                continue

            x1_ev_s = {}
            x1_ev_v = {}
            for tb in range(TBN):
                for dcg in range(2):
                    b = tb * 2 + dcg
                    last = None
                    for j in range(4):
                        dc = dcg * 4 + j
                        wl = ([ln1_x1[tb]] + bankg[b]) if j == 0 else ()
                        last = em.op("tensor",
                                     (lambda t, bb=b, jj=j, d=dc, tt=tb:
                                      t.transpose(PS[bb][:, jj * 128:jj * 128 + 128],
                                                  sslice(X1TM, tt * 1024 + d * 128, 128),
                                                  CIDN[:])),
                                     waits=wl, inc=("sT" if j == 3 else None))
                    off = dcg * 4 * 512 + tb * 128
                    mh = sop(lambda s, bb=b, o=off: s.copy(
                        bass.AP(KH, o, [[4096, 128], [512, 4], [1, 128]]),
                        bass.AP(PS[bb], 0, [[512, 128], [128, 4], [1, 128]])),
                        waits=[last, g_KH_free, s_last_mm])
                    ml = vop(lambda v, bb=b, o=off: v.tensor_tensor(
                        bass.AP(KL, o, [[4096, 128], [512, 4], [1, 128]]),
                        bass.AP(PS[bb], 0, [[512, 128], [128, 4], [1, 128]]),
                        bass.AP(KH, o, [[4096, 128], [512, 4], [1, 128]]),
                        op=OP.subtract),
                        waits=[last, mh, g_KH_free, s_last_mm])
                    bankg[b] = [mh, ml]
                    x1_ev_s[(tb, dcg)] = mh
                    x1_ev_v[(tb, dcg)] = ml
            x1_ready_s = later(*x1_ev_s.values())
            x1_ready_v = later(*x1_ev_v.values())

            if STAGE <= 5:
                _stage_exit(x1_ready_v, KH, KL)
                continue

            if not FFN_ON:
                for tb in range(TBN):
                    mo = gdma(lambda g, o=(tok0 + tb * 128) * D, t=tb: g.dma_start(
                        out=bass.AP(out_h, o, [[D, 128], [1, D]]),
                        in_=sslice(X1TM, t * 1024, 1024)),
                        waits=[ln1_x1[tb]], inc="sO")
                    out_ms.append(mo)
                g_A32_free = later(*ln1_x1.values())
                g_XT_free = out_ms[-1]
                g_X1_free = out_ms[-1]
                g_QH_free = z_mm_last
                g_KH_free = None
                g_VH_free = later(*o_done.values())
                g_FU_free = v_mm_last
                continue

            w_ready = {}
            for tb in range(TBN):
                b = tb % 2
                last = None
                for kc in range(KC):
                    wl = ([x1_ev_s[(tb, 0)], x1_ev_s[(tb, 1)],
                           x1_ev_v[(tb, 0)], x1_ev_v[(tb, 1)],
                           rt_ready_s, rt_ready_v] + bankg[b]) if kc == 0 else ()
                    last = mm3(PS[b][:, 0:8],
                               sslice(KH, kc * 512 + tb * 128, 128),
                               sslice(KL, kc * 512 + tb * 128, 128),
                               sslice(RT_HI, kc * 8, 8),
                               sslice(RT_LO, kc * 8, 8),
                               kc == 0, kc == KC - 1,
                               waits=wl, inc=("sT" if kc == KC - 1 else None))
                r = tb * 40
                pm = [None]

                def ch(fn, extra=()):
                    pm[0] = vop(fn, waits=[pm[0]] + list(extra))
                    return pm[0]

                ch(lambda v, bb=b, rr=r: v.reduce_max(RTI[:, rr + 32:rr + 33],
                                                      PS[bb][:, 0:8], axis=AXX),
                   extra=[last])
                mneg = ch(lambda v, rr=r: v.tensor_scalar(RTI[:, rr + 33:rr + 34],
                                                          RTI[:, rr + 32:rr + 33],
                                                          -1.0, None, op0=OP.mult))
                mexp = sop(lambda s, bb=b, rr=r: s.activation(RTI[:, rr:rr + 8],
                                                              PS[bb][:, 0:8], AF.Exp,
                                                              bias=RTI[:, rr + 33:rr + 34]),
                           waits=[mneg])
                bankg[b] = [mexp]
                ch(lambda v, rr=r: v.reduce_sum(RTI[:, rr + 35:rr + 36],
                                                RTI[:, rr:rr + 8], axis=AXX),
                   extra=[mexp])
                ch(lambda v, rr=r: v.reciprocal(RTI[:, rr + 36:rr + 37],
                                                RTI[:, rr + 35:rr + 36]))
                ch(lambda v, rr=r: v.tensor_scalar(RTI[:, rr:rr + 8], RTI[:, rr:rr + 8],
                                                   RTI[:, rr + 36:rr + 37], None,
                                                   op0=OP.mult))
                ch(lambda v, rr=r: v.reduce_max(RTI[:, rr + 32:rr + 33],
                                                RTI[:, rr:rr + 8], axis=AXX))
                mneg1 = ch(lambda v, rr=r: v.tensor_scalar(RTI[:, rr + 33:rr + 34],
                                                           RTI[:, rr + 32:rr + 33],
                                                           -1.0, None, op0=OP.mult))
                ch(lambda v, rr=r: v.tensor_scalar(RTI[:, rr + 8:rr + 16], RTI[:, rr:rr + 8],
                                                   RTI[:, rr + 32:rr + 33], None,
                                                   op0=OP.is_equal))
                ch(lambda v, rr=r: v.reduce_sum(RTI[:, rr + 37:rr + 38],
                                                RTI[:, rr + 8:rr + 16], axis=AXX))
                ch(lambda v, rr=r: v.scalar_tensor_tensor(RTI[:, rr + 16:rr + 24],
                                                          RTI[:, rr + 8:rr + 16],
                                                          -BIG, RTI[:, rr:rr + 8],
                                                          op0=OP.mult, op1=OP.add))
                ch(lambda v, rr=r: v.reduce_max(RTI[:, rr + 38:rr + 39],
                                                RTI[:, rr + 16:rr + 24], axis=AXX))
                ch(lambda v, rr=r: v.tensor_scalar(RTI[:, rr + 37:rr + 38],
                                                   RTI[:, rr + 37:rr + 38],
                                                   2.0, None, op0=OP.is_ge))
                ch(lambda v, rr=r: v.tensor_tensor(RTI[:, rr + 39:rr + 40],
                                                   RTI[:, rr + 32:rr + 33],
                                                   RTI[:, rr + 38:rr + 39],
                                                   op=OP.subtract))
                ch(lambda v, rr=r: v.tensor_tensor(RTI[:, rr + 39:rr + 40],
                                                   RTI[:, rr + 39:rr + 40],
                                                   RTI[:, rr + 37:rr + 38],
                                                   op=OP.mult))
                ch(lambda v, rr=r: v.tensor_tensor(RTI[:, rr + 38:rr + 39],
                                                   RTI[:, rr + 38:rr + 39],
                                                   RTI[:, rr + 39:rr + 40],
                                                   op=OP.add))
                ch(lambda v, rr=r: v.tensor_scalar(RTI[:, rr + 8:rr + 16], RTI[:, rr:rr + 8],
                                                   RTI[:, rr + 38:rr + 39], None,
                                                   op0=OP.is_ge))
                m2m1 = ch(lambda v, rr=r: v.tensor_tensor(RTI[:, rr + 39:rr + 40],
                                                          RTI[:, rr + 38:rr + 39],
                                                          RTI[:, rr + 32:rr + 33],
                                                          op=OP.subtract))
                me2 = sop(lambda s, rr=r: s.activation(RTI[:, rr + 39:rr + 40],
                                                       RTI[:, rr + 39:rr + 40], AF.Exp),
                          waits=[m2m1])
                men = sop(lambda s, rr=r: s.activation(RTI[:, rr + 16:rr + 24],
                                                       RTI[:, rr:rr + 8], AF.Exp,
                                                       bias=RTI[:, rr + 33:rr + 34]),
                          waits=[mneg1])
                ch(lambda v, rr=r: v.tensor_scalar(RTI[:, rr + 39:rr + 40],
                                                   RTI[:, rr + 39:rr + 40],
                                                   1.0, None, op0=OP.add),
                   extra=[me2])
                ch(lambda v, rr=r: v.reciprocal(RTI[:, rr + 36:rr + 37],
                                                RTI[:, rr + 39:rr + 40]))
                ch(lambda v, rr=r: v.tensor_tensor(RTI[:, rr + 16:rr + 24],
                                                   RTI[:, rr + 16:rr + 24],
                                                   RTI[:, rr + 8:rr + 16], op=OP.mult),
                   extra=[men])
                mw = ch(lambda v, rr=r, t=tb: v.tensor_scalar(
                    W32[:, t * 8:t * 8 + 8], RTI[:, rr + 16:rr + 24],
                    RTI[:, rr + 36:rr + 37], None, op0=OP.mult))
                w_ready[tb] = mw

            if STAGE <= 6:
                _stage_exit(later(*w_ready.values()))
                continue

            down_last_half = {}
            fuse_ms = {}
            gu_mm_last = [None]

            def emit_GU(e, hf):
                for fci in range(FH):
                    fc = hf * FH + fci
                    slg, fg = ring_kg.load(
                        ("kg", e, fc),
                        bass.AP(skg_hi, e * 128 * KC * F + fc * 1024, [[KC * F, 128], [1, 1024]]),
                        bass.AP(skg_lo, e * 128 * KC * F + fc * 1024, [[KC * F, 128], [1, 1024]]))
                    slu, fu = ring_ku.load(
                        ("ku", e, fc),
                        bass.AP(sku_hi, e * 128 * KC * F + fc * 1024, [[KC * F, 128], [1, 1024]]),
                        bass.AP(sku_lo, e * 128 * KC * F + fc * 1024, [[KC * F, 128], [1, 1024]]))
                    bg = 4 + (fc % 2)
                    bu = 6 + (fc % 2)
                    lg = lu = None
                    for kc in range(KC):
                        wl = ([fg, x1_ready_s, x1_ready_v] + bankg[bg]) if kc == 0 else ()
                        lg = mm3(PS[bg][:, 0:512],
                                 sslice(KGH, slg * 1024 + kc * 128, 128),
                                 sslice(KGL, slg * 1024 + kc * 128, 128),
                                 sslice(KH, kc * 512, 512), sslice(KL, kc * 512, 512),
                                 kc == 0, kc == KC - 1,
                                 waits=wl, inc=("sT" if kc == KC - 1 else None))
                    ring_kg.consumed(slg, lg)
                    for kc in range(KC):
                        wl = ([fu] + bankg[bu]) if kc == 0 else ()
                        lu = mm3(PS[bu][:, 0:512],
                                 sslice(KUH, slu * 1024 + kc * 128, 128),
                                 sslice(KUL, slu * 1024 + kc * 128, 128),
                                 sslice(KH, kc * 512, 512), sslice(KL, kc * 512, 512),
                                 kc == 0, kc == KC - 1,
                                 waits=wl, inc=("sT" if kc == KC - 1 else None))
                    ring_ku.consumed(slu, lu)
                    gu_mm_last[0] = lu
                    par = fc % 2
                    if e == 0:
                        prev = v_mm_last if hf == 0 else g_FU_free
                    else:
                        prev = down_last_half.get((e - 1, hf))
                    msil = sop(lambda s, p=par, bb=bg: s.activation(
                        sslice(GA32, p * 512, 512), PS[bb][:, 0:512], AF.Silu),
                        waits=[lg, ga_guard[par]])
                    bankg[bg] = [msil]
                    mfu = vop(lambda v, p=par, bb=bu: v.tensor_tensor(
                        sslice(FU32, p * 512, 512), sslice(GA32, p * 512, 512),
                        PS[bb][:, 0:512], op=OP.mult),
                        waits=[msil, lu, fu_guard[par]])
                    bankg[bu] = [mfu]
                    ga_guard[par] = mfu
                    mfh = sop(lambda s, p=par, f=fc: s.copy(
                        sslice(FUH, f * 512, 512), sslice(FU32, p * 512, 512)),
                        waits=[mfu, prev])
                    fu_guard[par] = mfh
                    mfl = vop(lambda v, p=par, f=fc: v.tensor_tensor(
                        sslice(FUL, f * 512, 512), sslice(FU32, p * 512, 512),
                        sslice(FUH, f * 512, 512), op=OP.subtract),
                        waits=[mfh, prev])
                    fuse_ms[(e, fc)] = (mfh, mfl)

            def emit_DOWN(e, hf):
                grp_last = {}
                last = None
                for fci in range(FH):
                    fc = hf * FH + fci
                    sld, fd = ring_kd.load(
                        ("kd", e, fc),
                        bass.AP(skd_hi, e * 128 * FC * D + fc * 1024, [[FC * D, 128], [1, 1024]]),
                        bass.AP(skd_lo, e * 128 * FC * D + fc * 1024, [[FC * D, 128], [1, 1024]]))
                    for nh in range(2):
                        for tb in range(TBN):
                            b = nh * 4 + tb
                            wl = [fd, fuse_ms[(e, fc)][0], fuse_ms[(e, fc)][1]]
                            if fci == 0:
                                wl = wl + bankg[b]
                            want_inc = (fci == FH - 1) or (nh == 1 and tb == TBN - 1)
                            last = mm3(PS[b][:, 0:512],
                                       sslice(FUH, fc * 512 + tb * 128, 128),
                                       sslice(FUL, fc * 512 + tb * 128, 128),
                                       sslice(KDH, sld * 1024 + nh * 512, 512),
                                       sslice(KDL, sld * 1024 + nh * 512, 512),
                                       fci == 0, fci == FH - 1,
                                       waits=wl,
                                       inc=("sT" if want_inc else None))
                            if fci == FH - 1:
                                grp_last[(nh, tb)] = last
                    ring_kd.consumed(sld, last)
                down_last_half[(e, hf)] = later(*grp_last.values())
                for nh in range(2):
                    for tb in range(TBN):
                        b = nh * 4 + tb
                        md = vop(lambda v, n=nh, t=tb, bb=b, ee=e: v.scalar_tensor_tensor(
                            sslice(X1TM, t * 1024 + n * 512, 512),
                            PS[bb][:, 0:512], W32[:, t * 8 + ee:t * 8 + ee + 1],
                            sslice(X1TM, t * 1024 + n * 512, 512),
                            op0=OP.mult, op1=OP.add),
                            waits=[grp_last[(nh, tb)], w_ready[tb]])
                        bankg[b] = [md]

            emit_GU(0, 0)
            emit_GU(0, 1)
            for e in range(E):
                emit_DOWN(e, 0)
                if e + 1 < E:
                    emit_GU(e + 1, 0)
                emit_DOWN(e, 1)
                if e + 1 < E:
                    emit_GU(e + 1, 1)

            down_all = later(*[bankg[i][0] for i in range(8)])
            ln2_done = []
            x2_ms = {}
            for tb in range(TBN):
                c = tb * 1024
                m0 = vop(lambda v, cc=c, t=tb: v.reduce_sum(STT[:, 32 + t:33 + t],
                                                            sslice(X1TM, cc, 1024), axis=AXX),
                         waits=[down_all])
                m1 = vop(lambda v, t=tb: v.tensor_scalar(STT[:, 32 + t:33 + t],
                                                         STT[:, 32 + t:33 + t],
                                                         1.0 / D, None, op0=OP.mult),
                         waits=[m0])
                mhm2 = vop(lambda v, cc=c, t=tb: v.tensor_scalar(
                    sslice(XT32, cc, 1024), sslice(X1TM, cc, 1024),
                    STT[:, 32 + t:33 + t], None, op0=OP.subtract), waits=[m1])
                msqa2 = sop(lambda s, cc=c, t=tb: s.activation(
                    sslice(A32, cc, 1024), sslice(XT32, cc, 1024), AF.Square),
                    waits=[mhm2])
                ln2_done.append(msqa2)
                m2 = vop(lambda v, cc=c, t=tb: v.reduce_sum(STT[:, 40 + t:41 + t],
                                                            sslice(A32, cc, 1024), axis=AXX),
                         waits=[msqa2])
                mv = vop(lambda v, t=tb: v.tensor_scalar(STT[:, 40 + t:41 + t],
                                                         STT[:, 40 + t:41 + t],
                                                         1.0 / D, None, op0=OP.mult),
                         waits=[m2])
                msq = sop(lambda s, t=tb: s.activation(STT[:, 48 + t:49 + t],
                                                       STT[:, 40 + t:41 + t],
                                                       AF.Sqrt, bias=LDC[:, 3:4]),
                          waits=[mv, c_eps])
                mr = vop(lambda v, t=tb: v.reciprocal(STT[:, 56 + t:57 + t],
                                                      STT[:, 48 + t:49 + t]),
                         waits=[msq])
                mx2 = vop(lambda v, cc=c, t=tb: v.tensor_scalar(
                    sslice(XT32, cc, 1024), sslice(XT32, cc, 1024),
                    STT[:, 56 + t:57 + t], None, op0=OP.mult), waits=[mr])
                x2_ms[tb] = mx2
            for tb in range(TBN):
                mo = gdma(lambda g, o=(tok0 + tb * 128) * D, t=tb: g.dma_start(
                    out=bass.AP(out_h, o, [[D, 128], [1, D]]),
                    in_=sslice(XT32, t * 1024, 1024)),
                    waits=[x2_ms[tb]], inc="sO")
                out_ms.append(mo)

            g_A32_free = later(*ln2_done)
            g_XT_free = out_ms[-1]
            g_X1_free = None
            g_QH_free = z_mm_last
            g_KH_free = gu_mm_last[0]
            g_VH_free = later(*o_done.values())
            g_FU_free = down_last_half[(E - 1, 1)]

        em.op("gpsimd", lambda g: g.nop(), waits=[("sO", em.counts["sO"])])

        def runner(engname):
            prog = em.progs[engname]

            def go(eng):
                for (waits, fn, mark, n) in prog:
                    for (s, v) in waits:
                        eng.wait_ge(semh[s], v)
                    instr = fn(eng)
                    if mark is not None:
                        instr.then_inc(semh[mark[0]], n)
            return go

        block.sync(runner("sync"))
        block.gpsimd(runner("gpsimd"))
        block.scalar(runner("scalar"))
        block.vector(runner("vector"))
        block.tensor(runner("tensor"))

    return nc


def _host_consts():
    p = np.arange(128)
    x = np.arange(128)
    dmat = (x[None, :] % 32 - p[:, None] % 32).astype(np.float32)
    bmat = ((x[None, :] // 32 == p[:, None] // 32) &
            (x[None, :] % 32 >= p[:, None] % 32)).astype(np.float32)
    return np.eye(128, dtype=np.float32), dmat, bmat


def kernel(L0_attn_x, L0_attn_W_q, L0_attn_W_k, L0_attn_W_v, L0_attn_W_g,
           L0_attn_W_o, L0_attn_log_decay, L0_ffn_router_kernel_DE,
           L0_ffn_kernel_gating_EDF, L0_ffn_kernel_up_EDF, L0_ffn_kernel_down_EFD):
    x = np.ascontiguousarray(np.asarray(L0_attn_x, dtype=np.float32)).reshape(NCORE * TOK, D)
    ident, dmat, bmat = _host_consts()
    shared = {
        "wqT": np.ascontiguousarray(np.asarray(L0_attn_W_q, np.float32).T),
        "wkT": np.ascontiguousarray(np.asarray(L0_attn_W_k, np.float32).T),
        "wgT": np.ascontiguousarray(np.asarray(L0_attn_W_g, np.float32).T),
        "wvT": np.ascontiguousarray(np.asarray(L0_attn_W_v, np.float32).T),
        "woT": np.ascontiguousarray(np.asarray(L0_attn_W_o, np.float32).T),
        "router": np.ascontiguousarray(np.asarray(L0_ffn_router_kernel_DE, np.float32)),
        "kg": np.ascontiguousarray(np.asarray(L0_ffn_kernel_gating_EDF, np.float32)),
        "ku": np.ascontiguousarray(np.asarray(L0_ffn_kernel_up_EDF, np.float32)),
        "kd": np.ascontiguousarray(np.asarray(L0_ffn_kernel_down_EFD, np.float32)),
        "log_decay": np.ascontiguousarray(np.asarray(L0_attn_log_decay, np.float32)),
        "ident": ident, "dmat": dmat, "bmat": bmat,
    }
    in_maps = []
    for c in range(NCORE):
        xs = x[c * TOK:(c + 1) * TOK]
        m = dict(shared)
        m["x_tm"] = np.ascontiguousarray(xs)
        m["x_fm"] = np.ascontiguousarray(xs.T)
        in_maps.append(m)
    nc = build()
    res = run_bass_kernel_spmd(nc, in_maps, core_ids=list(range(NCORE)))
    outs = [np.asarray(res.results[c]["out"]) for c in range(NCORE)]
    return np.concatenate(outs, axis=0).reshape(2, 8192, 1024)


# version 05 (final)
# speedup vs baseline: 1.1741x; 1.1741x over previous
import numpy as np
import ml_dtypes
import concourse.bass as bass
import concourse.mybir as mybir
from concourse.bass_utils import run_bass_kernel_spmd
from concourse.alu_op_type import AluOpType as OP
from contextlib import ExitStack

F32 = mybir.dt.float32
BF16 = mybir.dt.bfloat16
AF = mybir.ActivationFunctionType
AXX = mybir.AxisListType.X

NCORE = 8
TOK = 2048
NT = 4
T = 512
TBN = 4
D = 1024
KC = 8
E = 8
F = 2048
FC = 16
FH = 8
BIG = 1.0e9
EPS = 1e-5

ENGINES = ("sync", "gpsimd", "scalar", "vector", "tensor")
SEMS = ("sW", "sG", "sO", "sT", "sV", "sS", "sX")

FFN_ON = True
RING_SWDGE = False
STAGE = 99


class Em:

    def __init__(self):
        self.progs = {e: [] for e in ENGINES}
        self.counts = {s: 0 for s in SEMS}
        self.last_wait = {e: {} for e in ENGINES}

    def op(self, eng, fn, waits=(), inc=None, inc_n=1):
        w = []
        lw = self.last_wait[eng]
        for sv in waits:
            if sv is None:
                continue
            s, v = sv
            assert v <= self.counts[s], f"wait on future {s}>={v} (now {self.counts[s]})"
            if lw.get(s, 0) < v:
                lw[s] = v
                w.append((s, v))
        mark = None
        if inc is not None:
            self.counts[inc] += inc_n
            mark = (inc, self.counts[inc])
        self.progs[eng].append((w, fn, mark, inc_n))
        return mark


def later(*ms):
    best = None
    for m in ms:
        if m is None:
            continue
        if best is None:
            best = m
        else:
            assert m[0] == best[0], (m, best)
            if m[1] > best[1]:
                best = m
    return best


def build():
    nc = bass.Bass("TRN2", target_bir_lowering=False, detect_race_conditions=False)

    xfm_h = nc.dram_tensor("x_fm", [D, TOK], F32, kind="ExternalInput")
    xtm_h = nc.dram_tensor("x_tm", [TOK, D], F32, kind="ExternalInput")
    w_h = [nc.dram_tensor(n, [128, KC * D], F32, kind="ExternalInput")
           for n in ("wqT", "wkT", "wgT", "wvT", "woT")]
    rt_h = nc.dram_tensor("router", [128, KC * E], F32, kind="ExternalInput")
    kg_h = nc.dram_tensor("kg", [E, 128, KC * F], F32, kind="ExternalInput")
    ku_h = nc.dram_tensor("ku", [E, 128, KC * F], F32, kind="ExternalInput")
    kd_h = nc.dram_tensor("kd", [E, 128, FC * D], F32, kind="ExternalInput")
    ld_h = nc.dram_tensor("log_decay", [D], F32, kind="ExternalInput")
    idn_h = nc.dram_tensor("ident", [128, 128], F32, kind="ExternalInput")
    dd_h = nc.dram_tensor("dmat", [128, 128], F32, kind="ExternalInput")
    bb_h = nc.dram_tensor("bmat", [128, 128], F32, kind="ExternalInput")
    out_h = nc.dram_tensor("out", [TOK, D], F32, kind="ExternalOutput")
    dbh_h = nc.dram_tensor("dbg_hi", [128, 8192], BF16, kind="ExternalOutput")
    dbl_h = nc.dram_tensor("dbg_lo", [128, 8192], BF16, kind="ExternalOutput")
    dbf_h = nc.dram_tensor("dbg_f32", [128, 4096], F32, kind="ExternalOutput")
    dbs_h = nc.dram_tensor("dbg_stt", [128, 64], F32, kind="ExternalOutput")

    swa_hi = nc.dram_tensor("swa_hi", [5, 128, KC * D], BF16)
    swa_lo = nc.dram_tensor("swa_lo", [5, 128, KC * D], BF16)
    skg_hi = nc.dram_tensor("skg_hi", [E, 128, KC * F], BF16)
    skg_lo = nc.dram_tensor("skg_lo", [E, 128, KC * F], BF16)
    sku_hi = nc.dram_tensor("sku_hi", [E, 128, KC * F], BF16)
    sku_lo = nc.dram_tensor("sku_lo", [E, 128, KC * F], BF16)
    skd_hi = nc.dram_tensor("skd_hi", [E, 128, FC * D], BF16)
    skd_lo = nc.dram_tensor("skd_lo", [E, 128, FC * D], BF16)

    with ExitStack() as st:
        def sb(n, shape, dt):
            return st.enter_context(nc.sbuf_tensor(n, shape, dt))

        A32 = sb("A32", [128, 4096], F32)
        XT32 = sb("XT32", [128, 4096], F32)
        X1TM = sb("X1TM", [128, 4096], F32)
        GA32 = sb("GA32", [128, 1024], F32)
        FU32 = sb("FU32", [128, 1024], F32)
        RTI = sb("RTI", [128, 160], F32)
        STT = sb("STT", [128, 64], F32)
        W32 = sb("W32", [128, 32], F32)
        LDC = sb("LDC", [128, 4], F32)
        CIDN = sb("CIDN", [128, 128], F32)
        CMSK = sb("CMSK", [128, 128], F32)
        RT_HI = sb("RT_HI", [128, 64], BF16)
        RT_LO = sb("RT_LO", [128, 64], BF16)
        SB32 = sb("SB32", [128, 128], F32)
        SBH = sb("SBH", [128, 128], BF16)
        SBL = sb("SBL", [128, 128], BF16)
        QH = sb("QH", [128, 4096], BF16)
        QL = sb("QL", [128, 4096], BF16)
        KH = sb("KH", [128, 4096], BF16)
        KL = sb("KL", [128, 4096], BF16)
        VH = sb("VH", [128, 4096], BF16)
        VL = sb("VL", [128, 4096], BF16)
        FUH = sb("FUH", [128, 8192], BF16)
        FUL = sb("FUL", [128, 8192], BF16)
        AWH = sb("AWH", [128, 2048], BF16)
        AWL = sb("AWL", [128, 2048], BF16)
        KGH = sb("KGH", [128, 3072], BF16)
        KGL = sb("KGL", [128, 3072], BF16)
        KUH = sb("KUH", [128, 3072], BF16)
        KUL = sb("KUL", [128, 3072], BF16)
        KDH = sb("KDH", [128, 3072], BF16)
        KDL = sb("KDL", [128, 3072], BF16)
        P0F = sb("P0F", [128, 3072], F32)
        P0H = sb("P0H", [128, 3072], BF16)
        P0L = sb("P0L", [128, 3072], BF16)

        PS = [st.enter_context(nc.psum_tensor(f"ps{i}", [128, 512], F32)) for i in range(8)]
        semh = {s: st.enter_context(nc.semaphore(s)) for s in SEMS}
        block = st.enter_context(nc.Block())

        em = Em()
        bankg = [[] for _ in range(8)]

        def mm(out, lh, rh, start, stop, waits=(), inc=None):
            return em.op(
                "tensor",
                (lambda t, o=out, l=lh, r=rh, s=start, p=stop:
                 t.matmul(o, l, r, start=s, stop=p)),
                waits=waits, inc=inc)

        def mm3(out, lh, ll, rh, rl, first, last, waits=(), inc=None):
            mm(out, lh, rh, first, False, waits=waits)
            mm(out, lh, rl, False, False)
            return mm(out, ll, rh, False, last, inc=inc)

        def vop(fn, waits=(), inc="sV"):
            return em.op("vector", fn, waits=waits, inc=inc)

        def sop(fn, waits=(), inc="sS"):
            return em.op("scalar", fn, waits=waits, inc=inc)

        def gdma(fn, waits=(), inc="sG"):
            return em.op("gpsimd", fn, waits=waits, inc=inc, inc_n=16)

        def wdma(fn, waits=()):
            return em.op("sync", fn, waits=waits, inc="sW", inc_n=16)

        def sslice(t, c0, n):
            return t[:, c0:c0 + n]

        c_idn = gdma(lambda g: g.dma_start(out=CIDN[:], in_=idn_h.ap()))
        c_dd = gdma(lambda g: g.dma_start(out=A32[:, 0:128], in_=dd_h.ap()))
        c_bb = gdma(lambda g: g.dma_start(out=A32[:, 128:256], in_=bb_h.ap()))
        c_ld = gdma(lambda g: g.dma_start(out=LDC[:, 0:1],
                                          in_=bass.AP(ld_h, 0, [[0, 128], [1, 1]])))
        c_rt = gdma(lambda g: g.dma_start(
            out=bass.AP(P0F, 0, [[3072, 128], [1, KC * E]]),
            in_=bass.AP(rt_h, 0, [[KC * E, 128], [1, KC * E]])))
        m = sop(lambda s: s.copy(RT_HI[:], P0F[:, 0:64]), waits=[c_rt])
        rt_ready_s = m
        rt_ready_v = vop(lambda v: v.tensor_tensor(RT_LO[:], P0F[:, 0:64], RT_HI[:],
                                                   op=OP.subtract), waits=[c_rt, m])
        m = sop(lambda s: s.activation(LDC[:, 1:2], LDC[:, 0:1], AF.Sigmoid), waits=[c_ld])
        m = sop(lambda s: s.activation(LDC[:, 2:3], LDC[:, 1:2], AF.Ln), waits=[m])
        m = sop(lambda s: s.activation(CMSK[:], A32[:, 0:128], AF.Exp, scale=LDC[:, 2:3]),
                waits=[c_dd, m])
        msk_ready = vop(lambda v: v.tensor_tensor(CMSK[:], CMSK[:], A32[:, 128:256], op=OP.mult),
                        waits=[c_bb, m])
        c_eps = em.op("gpsimd", lambda g: g.memset(LDC[:, 3:4], EPS), inc="sX", inc_n=1)

        P0N = 3
        p0_slot_free = [[rt_ready_s, rt_ready_v], [], []]
        p0_idx = [0]
        scratch_ready = {}
        p0_queue = []

        def phase0_unit(key, loads, out_hi_ap_fn, out_lo_ap_fn):
            sl = p0_idx[0] % P0N
            p0_idx[0] += 1
            c0 = sl * 1024
            ld = None
            first = True
            for (a, co, wd) in loads:
                ld = gdma(lambda g, aa=a, c=c0 + co, ww=wd: g.dma_start(
                    out=bass.AP(P0F, c, [[3072, 128], [1, ww]]), in_=aa),
                    waits=(list(p0_slot_free[sl]) if first else ()))
                first = False
            hi = sop(lambda s, c=c0: s.copy(
                bass.AP(P0H, c, [[3072, 128], [1, 1024]]),
                bass.AP(P0F, c, [[3072, 128], [1, 1024]])), waits=[ld])
            lo = vop(lambda v, c=c0: v.tensor_tensor(
                bass.AP(P0L, c, [[3072, 128], [1, 1024]]),
                bass.AP(P0F, c, [[3072, 128], [1, 1024]]),
                bass.AP(P0H, c, [[3072, 128], [1, 1024]]), op=OP.subtract),
                waits=[ld, hi])
            gdma(lambda g, c=c0, f=out_hi_ap_fn: g.dma_start(
                out=f(), in_=bass.AP(P0H, c, [[3072, 128], [1, 1024]])),
                waits=[hi, lo])
            wr = gdma(lambda g, c=c0, f=out_lo_ap_fn: g.dma_start(
                out=f(), in_=bass.AP(P0L, c, [[3072, 128], [1, 1024]])))
            p0_slot_free[sl] = [wr]
            scratch_ready[key] = wr

        def p0_enqueue(*args):
            p0_queue.append(args)

        p0_done = [0]

        def p0_pump(n):
            k = min(n, len(p0_queue))
            for _ in range(k):
                phase0_unit(*p0_queue.pop(0))
            p0_done[0] += k

        def p0_pump_until(total):
            if p0_done[0] < total:
                p0_pump(total - p0_done[0])

        for w in range(5):
            for u in range(KC):
                loads = [(bass.AP(w_h[w], u * 1024, [[KC * D, 128], [1, 1024]]), 0, 1024)]
                off = w * 128 * (KC * D) + u * 1024
                hi_fn = (lambda o=off: bass.AP(swa_hi, o, [[KC * D, 128], [1, 1024]]))
                lo_fn = (lambda o=off: bass.AP(swa_lo, o, [[KC * D, 128], [1, 1024]]))
                phase0_unit(("wa", w, u), loads, hi_fn, lo_fn)
            scratch_ready[("wa", w)] = scratch_ready[("wa", w, KC - 1)]

        if FFN_ON:
            for e in range(E):
                for mat, mh_, shi, slo in (("kg", kg_h, skg_hi, skg_lo),
                                           ("ku", ku_h, sku_hi, sku_lo)):
                    for fc in range(FC):
                        off = e * 128 * (KC * F) + fc * 1024
                        loads = [(bass.AP(mh_, off, [[KC * F, 128], [1, 1024]]), 0, 1024)]
                        hi_fn = (lambda o=off, t=shi: bass.AP(t, o, [[KC * F, 128], [1, 1024]]))
                        lo_fn = (lambda o=off, t=slo: bass.AP(t, o, [[KC * F, 128], [1, 1024]]))
                        p0_enqueue((mat, e, fc), loads, hi_fn, lo_fn)
                for fc in range(FC):
                    off = e * 128 * (FC * D) + fc * 1024
                    loads = [(bass.AP(kd_h, off, [[FC * D, 128], [1, 1024]]), 0, 1024)]
                    hi_fn = (lambda o=off: bass.AP(skd_hi, o, [[FC * D, 128], [1, 1024]]))
                    lo_fn = (lambda o=off: bass.AP(skd_lo, o, [[FC * D, 128], [1, 1024]]))
                    p0_enqueue(("kd", e, fc), loads, hi_fn, lo_fn)

        class Ring:
            def __init__(self, hi_sb, lo_sb, nslots, width):
                self.hi, self.lo, self.n, self.w = hi_sb, lo_sb, nslots, width
                self.lastread = [None] * nslots
                self.i = 0

            def load(self, key, src_hi, src_lo):
                sl = self.i % self.n
                self.i += 1
                c0 = sl * 1024
                assert key in scratch_ready, f"scratch not written for {key}"
                dmaf = gdma if RING_SWDGE else wdma
                if RING_SWDGE:
                    dmaf = lambda fn, waits=(): em.op("gpsimd", fn, waits=waits, inc="sW", inc_n=16)
                dmaf(lambda y, c=c0, a=src_hi, t=self.hi, W=self.w: y.dma_start(
                    out=bass.AP(t, c, [[W, 128], [1, 1024]]), in_=a),
                    waits=[self.lastread[sl], scratch_ready.get(key)])
                f = dmaf(lambda y, c=c0, a=src_lo, t=self.lo, W=self.w: y.dma_start(
                    out=bass.AP(t, c, [[W, 128], [1, 1024]]), in_=a))
                return sl, f

            def consumed(self, sl, milestone):
                self.lastread[sl] = milestone

        ring_aw = Ring(AWH, AWL, 2, 2048)
        ring_kg = Ring(KGH, KGL, 3, 3072)
        ring_ku = Ring(KUH, KUL, 3, 3072)
        ring_kd = Ring(KDH, KDL, 3, 3072)

        ga_guard = {0: None, 1: None}
        fu_guard = {0: None, 1: None}

        g_A32_free = msk_ready
        g_XT_free = None
        g_X1_free = None
        g_QH_free = None
        g_KH_free = None
        g_VH_free = None
        g_FU_free = None
        out_ms = []

        for it in range(NT):
            tok0 = it * T
            ld_xfm = None
            for kc in range(KC):
                ld_xfm = gdma(lambda g, o=tok0, k=kc: g.dma_start(
                    out=bass.AP(A32, k * 512, [[4096, 128], [1, 512]]),
                    in_=bass.AP(xfm_h, k * 128 * TOK + o, [[TOK, 128], [1, 512]])),
                    waits=[g_A32_free])
            ld_xtm = None
            for tb in range(TBN):
                ld_xtm = gdma(lambda g, o=tok0 * D, t=tb: g.dma_start(
                    out=bass.AP(XT32, t * 1024, [[4096, 128], [1, 1024]]),
                    in_=bass.AP(xtm_h, (t * 128) * D + o, [[D, 128], [1, 1024]])),
                    waits=[g_XT_free])
            xs_hi = sop(lambda s: s.copy(sslice(FUH, 0, 4096), A32[:]),
                        waits=[ld_xfm, g_FU_free])
            xs_lo = vop(lambda v: v.tensor_tensor(sslice(FUL, 0, 4096), A32[:],
                                                  sslice(FUH, 0, 4096), op=OP.subtract),
                        waits=[ld_xfm, xs_hi, g_FU_free])


            def _stage_exit(last_v, hi_sb=None, lo_sb=None, f32_sb=None, n=4096):
                if it == 0:
                    if hi_sb is not None:
                        mo = gdma(lambda g, t=hi_sb, nn=n: g.dma_start(
                            out=bass.AP(dbh_h, 0, [[8192, 128], [1, nn]]),
                            in_=t[:, 0:nn]), waits=[last_v], inc="sO")
                        out_ms.append(mo)
                    if lo_sb is not None:
                        mo = gdma(lambda g, t=lo_sb, nn=n: g.dma_start(
                            out=bass.AP(dbl_h, 0, [[8192, 128], [1, nn]]),
                            in_=t[:, 0:nn]), waits=[last_v], inc="sO")
                        out_ms.append(mo)
                    if f32_sb is not None:
                        mo = gdma(lambda g, t=f32_sb: g.dma_start(
                            out=bass.AP(dbf_h, 0, [[4096, 128], [1, 4096]]),
                            in_=t[:, 0:4096]), waits=[last_v], inc="sO")
                        out_ms.append(mo)
                    src_s = W32 if STAGE == 6 else STT
                    mo = gdma(lambda g, t=src_s: g.dma_start(
                        out=bass.AP(dbs_h, 0, [[64, 128], [1, t.shape[1] if hasattr(t, "shape") else 64]][0:2]
                                    ) if False else bass.AP(dbs_h, 0, [[64, 128], [1, 32 if STAGE == 6 else 64]]),
                        in_=t[:, 0:(32 if STAGE == 6 else 64)]), waits=[last_v], inc="sO")
                    out_ms.append(mo)
                mo = gdma(lambda g, o=tok0 * D: g.dma_start(
                    out=bass.AP(out_h, o, [[D, 128], [1, D]]),
                    in_=sslice(XT32, 0, 1024)), waits=[last_v], inc="sO")
                out_ms.append(mo)

            if STAGE <= 1:
                _stage_exit(xs_lo, FUH, FUL)
                continue

            evac = {}
            for which, wi, HI, LO in (("q", 0, QH, QL), ("k", 1, KH, KL), ("g", 2, None, None)):
                for dc in range(KC):
                    sl, fill = ring_aw.load(
                        ("wa", wi),
                        bass.AP(swa_hi, wi * 128 * KC * D + dc * 1024, [[KC * D, 128], [1, 1024]]),
                        bass.AP(swa_lo, wi * 128 * KC * D + dc * 1024, [[KC * D, 128], [1, 1024]]))
                    b = dc % 4
                    last = None
                    for kc in range(KC):
                        wl = ([fill, xs_hi, xs_lo] + bankg[b]) if kc == 0 else ()
                        last = mm3(PS[b][:, 0:512],
                                   sslice(AWH, sl * 1024 + kc * 128, 128),
                                   sslice(AWL, sl * 1024 + kc * 128, 128),
                                   sslice(FUH, kc * 512, 512),
                                   sslice(FUL, kc * 512, 512),
                                   kc == 0, kc == KC - 1,
                                   waits=wl, inc=("sT" if kc == KC - 1 else None))
                    ring_aw.consumed(sl, last)
                    if which == "g":
                        mg = sop(lambda s, d=dc, bb=b: s.activation(
                            sslice(A32, d * 512, 512), PS[bb][:, 0:512], AF.Sigmoid),
                            waits=[last, xs_lo])
                        bankg[b] = [mg]
                        evac[("g", dc)] = (mg, None)
                    else:
                        free = g_QH_free if which == "q" else g_KH_free
                        mh = sop(lambda s, d=dc, bb=b, t=HI: s.copy(
                            sslice(t, d * 512, 512), PS[bb][:, 0:512]),
                            waits=[last, free])
                        ml = vop(lambda v, d=dc, bb=b, th=HI, tl=LO: v.tensor_tensor(
                            sslice(tl, d * 512, 512), PS[bb][:, 0:512],
                            sslice(th, d * 512, 512), op=OP.subtract),
                            waits=[last, mh, free])
                        bankg[b] = [mh, ml]
                        evac[(which, dc)] = (mh, ml)

            if STAGE <= 2:
                _stage_exit(later(*[evac[("q", dc)][1] for dc in range(KC)]), QH, QL)
                continue

            v_last = {}
            v_mm_last = None
            for kc in range(KC):
                sl, fill = ring_aw.load(
                    ("wa", 3),
                    bass.AP(swa_hi, 3 * 128 * KC * D + kc * 1024, [[KC * D, 128], [1, 1024]]),
                    bass.AP(swa_lo, 3 * 128 * KC * D + kc * 1024, [[KC * D, 128], [1, 1024]]))
                last = None
                for nh in range(2):
                    for tb in range(TBN):
                        b = nh * 4 + tb
                        wl = [fill] + (bankg[b] if kc == 0 else [])
                        want_inc = (kc == KC - 1) or (nh == 1 and tb == TBN - 1)
                        last = mm3(PS[b][:, 0:512],
                                   sslice(FUH, kc * 512 + tb * 128, 128),
                                   sslice(FUL, kc * 512 + tb * 128, 128),
                                   sslice(AWH, sl * 1024 + nh * 512, 512),
                                   sslice(AWL, sl * 1024 + nh * 512, 512),
                                   kc == 0, kc == KC - 1,
                                   waits=wl, inc=("sT" if want_inc else None))
                        if kc == KC - 1:
                            v_last[(nh, tb)] = last
                ring_aw.consumed(sl, last)
                v_mm_last = last
            v_ev = {}
            for nh in range(2):
                for tb in range(TBN):
                    b = nh * 4 + tb
                    mh = sop(lambda s, t=tb, n=nh, bb=b: s.copy(
                        sslice(VH, t * 1024 + n * 512, 512), PS[bb][:, 0:512]),
                        waits=[v_last[(nh, tb)], g_VH_free])
                    ml = vop(lambda v, t=tb, n=nh, bb=b: v.tensor_tensor(
                        sslice(VL, t * 1024 + n * 512, 512), PS[bb][:, 0:512],
                        sslice(VH, t * 1024 + n * 512, 512), op=OP.subtract),
                        waits=[v_last[(nh, tb)], mh, g_VH_free])
                    bankg[b] = [mh, ml]
                    v_ev[(nh, tb)] = (mh, ml)

            if STAGE <= 3:
                _stage_exit(later(*[v_ev[k][1] for k in v_ev]), VH, VL)
                continue

            qk_ready_s = later(*([evac[("q", dc)][0] for dc in range(KC)] +
                                 [evac[("k", dc)][0] for dc in range(KC)]))
            qk_ready_v = later(*([evac[("q", dc)][1] for dc in range(KC)] +
                                 [evac[("k", dc)][1] for dc in range(KC)]))
            v_ready_s = later(*[v_ev[k][0] for k in v_ev])
            v_ready_v = later(*[v_ev[k][1] for k in v_ev])
            gate_ready = later(*[evac[("g", dc)][0] for dc in range(KC)])

            s_done = {}
            sb_hi = {}
            sb_lo = {}
            o_done = {}
            oev = []

            def emit_S(g):
                b = g % 2
                last = None
                for dc in range(KC):
                    wl = ([qk_ready_s, qk_ready_v, msk_ready] + bankg[b]) if dc == 0 else ()
                    last = mm3(PS[b][:, 0:128],
                               sslice(KH, dc * 512 + g * 128, 128),
                               sslice(KL, dc * 512 + g * 128, 128),
                               sslice(QH, dc * 512 + g * 128, 128),
                               sslice(QL, dc * 512 + g * 128, 128),
                               dc == 0, dc == KC - 1,
                               waits=wl, inc=("sT" if dc == KC - 1 else None))
                s_done[g] = (last, b)

            def emit_S_evac(g, prev_o_last):
                last, b = s_done[g]
                wl = [last]
                if g - 1 in sb_hi:
                    wl.append(sb_hi[g - 1])
                mt = vop(lambda v, bb=b: v.tensor_tensor(SB32[:], PS[bb][:, 0:128],
                                                         CMSK[:], op=OP.mult), waits=wl)
                bankg[b] = [mt]
                mh = sop(lambda s: s.copy(SBH[:], SB32[:]), waits=[mt, prev_o_last])
                ml = vop(lambda v: v.tensor_tensor(SBL[:], SB32[:], SBH[:], op=OP.subtract),
                         waits=[mh, prev_o_last])
                sb_hi[g] = mh
                sb_lo[g] = ml
                return mh, ml

            def emit_O(g, mh, ml):
                for dc in range(KC):
                    b = 2 + (dc % 4)
                    wl = [mh, ml, v_ready_s, v_ready_v] + bankg[b]
                    last = mm3(PS[b][:, 0:128],
                               sslice(VH, g * 1024 + dc * 128, 128),
                               sslice(VL, g * 1024 + dc * 128, 128),
                               SBH[:], SBL[:],
                               True, True, waits=wl, inc="sT")
                    o_done[(g, dc)] = last
                    me = vop(lambda v, d=dc, gg=g, bb=b: v.tensor_tensor(
                        sslice(X1TM, d * 512 + gg * 128, 128), PS[bb][:, 0:128],
                        sslice(A32, d * 512 + gg * 128, 128), op=OP.mult),
                        waits=[last, gate_ready, g_X1_free])
                    bankg[b] = [me]
                    oev.append(me)

            emit_S(0)
            emit_S(1)
            sev = emit_S_evac(0, None)
            emit_O(0, *sev)
            emit_S(2)
            sev = emit_S_evac(1, o_done[(0, KC - 1)])
            emit_O(1, *sev)
            emit_S(3)
            sev = emit_S_evac(2, o_done[(1, KC - 1)])
            emit_O(2, *sev)
            sev = emit_S_evac(3, o_done[(2, KC - 1)])
            emit_O(3, *sev)

            y_ready = later(*oev)
            s_last_mm = s_done[3][0]
            y_hi = sop(lambda s: s.copy(QH[:], X1TM[:]),
                       waits=[y_ready, s_last_mm, g_QH_free])
            y_lo = vop(lambda v: v.tensor_tensor(QL[:], X1TM[:], QH[:], op=OP.subtract),
                       waits=[y_hi, s_last_mm, g_QH_free])

            if STAGE <= 4:
                _stage_exit(y_lo, QH, QL)
                continue

            z_ev = []
            z_mm_last = None
            for dc in range(KC):
                sl, fill = ring_aw.load(
                    ("wa", 4),
                    bass.AP(swa_hi, 4 * 128 * KC * D + dc * 1024, [[KC * D, 128], [1, 1024]]),
                    bass.AP(swa_lo, 4 * 128 * KC * D + dc * 1024, [[KC * D, 128], [1, 1024]]))
                b = dc % 4
                last = None
                for kc in range(KC):
                    wl = ([fill, y_hi, y_lo] + bankg[b]) if kc == 0 else ()
                    last = mm3(PS[b][:, 0:512],
                               sslice(AWH, sl * 1024 + kc * 128, 128),
                               sslice(AWL, sl * 1024 + kc * 128, 128),
                               sslice(QH, kc * 512, 512),
                               sslice(QL, kc * 512, 512),
                               kc == 0, kc == KC - 1,
                               waits=wl, inc=("sT" if kc == KC - 1 else None))
                ring_aw.consumed(sl, last)
                z_mm_last = last
                mz = vop(lambda v, d=dc, bb=b: v.tensor_copy(sslice(A32, d * 512, 512),
                                                             PS[bb][:, 0:512]),
                         waits=[last])
                bankg[b] = [mz]
                z_ev.append(mz)
            z_ready = later(*z_ev)
            if it == 0:
                p0_pump(40)
            if STAGE <= 4.2:
                _stage_exit(z_ready, f32_sb=A32)
                continue

            tp_done = {}
            for tb in range(TBN):
                for dcg in range(2):
                    b = tb * 2 + dcg
                    last = None
                    for j in range(4):
                        dc = dcg * 4 + j
                        wl = ([z_ready] + bankg[b]) if j == 0 else ()
                        last = em.op("tensor",
                                     (lambda t, bb=b, jj=j, d=dc, tt=tb:
                                      t.transpose(PS[bb][:, jj * 128:jj * 128 + 128],
                                                  sslice(A32, d * 512 + tt * 128, 128),
                                                  CIDN[:])),
                                     waits=wl, inc=("sT" if j == 3 else None))
                    tp_done[(tb, dcg)] = last
            tp_all = tp_done[(TBN - 1, 1)]
            h_st = {}
            for tb in range(TBN):
                for dcg in range(2):
                    b = tb * 2 + dcg
                    ms = vop(lambda v, t=tb, g2=dcg, bb=b: v.scalar_tensor_tensor(
                        sslice(A32, t * 1024 + g2 * 512, 512),
                        sslice(XT32, t * 1024 + g2 * 512, 512),
                        2.0, PS[bb][:, 0:512], op0=OP.mult, op1=OP.add),
                        waits=[tp_all, ld_xtm])
                    bankg[b] = [ms]
                    h_st[(tb, dcg)] = ms

            if STAGE <= 4.4:
                _stage_exit(later(*h_st.values()), f32_sb=A32)
                continue

            ln1_x1 = {}
            for tb in range(TBN):
                c = tb * 1024
                m0 = vop(lambda v, cc=c, t=tb: v.reduce_sum(STT[:, t:t + 1],
                                                            sslice(A32, cc, 1024), axis=AXX),
                         waits=[h_st[(tb, 0)], h_st[(tb, 1)]])
                m1 = vop(lambda v, t=tb: v.tensor_scalar(STT[:, t:t + 1], STT[:, t:t + 1],
                                                         1.0 / D, None, op0=OP.mult),
                         waits=[m0])
                mhm = vop(lambda v, cc=c, t=tb: v.tensor_scalar(
                    sslice(X1TM, cc, 1024), sslice(A32, cc, 1024),
                    STT[:, t:t + 1], None, op0=OP.subtract), waits=[m1])
                msqa = sop(lambda s, cc=c, t=tb: s.activation(
                    sslice(XT32, cc, 1024), sslice(X1TM, cc, 1024), AF.Square),
                    waits=[mhm])
                m2 = vop(lambda v, cc=c, t=tb: v.reduce_sum(STT[:, 8 + t:9 + t],
                                                            sslice(XT32, cc, 1024), axis=AXX),
                         waits=[msqa])
                mv = vop(lambda v, t=tb: v.tensor_scalar(STT[:, 8 + t:9 + t],
                                                         STT[:, 8 + t:9 + t],
                                                         1.0 / D, None, op0=OP.mult),
                         waits=[m2])
                msq = sop(lambda s, t=tb: s.activation(STT[:, 16 + t:17 + t],
                                                       STT[:, 8 + t:9 + t],
                                                       AF.Sqrt, bias=LDC[:, 3:4]),
                          waits=[mv, c_eps])
                mr = vop(lambda v, t=tb: v.reciprocal(STT[:, 24 + t:25 + t],
                                                      STT[:, 16 + t:17 + t]),
                         waits=[msq])
                mx1 = vop(lambda v, cc=c, t=tb: v.tensor_scalar(
                    sslice(X1TM, cc, 1024), sslice(X1TM, cc, 1024),
                    STT[:, 24 + t:25 + t], None, op0=OP.mult), waits=[mr])
                ln1_x1[tb] = mx1

            if STAGE <= 4.6:
                _stage_exit(later(*ln1_x1.values()), f32_sb=X1TM)
                continue

            tp2 = {}
            for dc in range(KC):
                b = dc
                last = None
                for tb in range(TBN):
                    wl = [ln1_x1[tb]] + (bankg[b] if tb == 0 else [])
                    last = em.op("tensor",
                                 (lambda t, bb=b, d=dc, tt=tb:
                                  t.transpose(PS[bb][:, tt * 128:tt * 128 + 128],
                                              sslice(X1TM, tt * 1024 + d * 128, 128),
                                              CIDN[:])),
                                 waits=wl, inc=("sT" if tb == TBN - 1 else None))
                tp2[dc] = last
            x1_ev_s = {}
            x1_ev_v = {}
            for dc in range(KC):
                b = dc
                wl_extra = tp2[dc + 1] if dc + 1 < KC else tp2[KC - 1]
                mh = sop(lambda s, bb=b, d=dc: s.copy(
                    sslice(KH, d * 512, 512), PS[bb][:, 0:512]),
                    waits=[tp2[dc], wl_extra, g_KH_free, s_last_mm])
                ml = vop(lambda v, bb=b, d=dc: v.tensor_tensor(
                    sslice(KL, d * 512, 512), PS[bb][:, 0:512],
                    sslice(KH, d * 512, 512), op=OP.subtract),
                    waits=[tp2[dc], wl_extra, mh, g_KH_free, s_last_mm])
                bankg[b] = [mh, ml]
                x1_ev_s[dc] = mh
                x1_ev_v[dc] = ml
            x1_ready_s = later(*x1_ev_s.values())
            x1_ready_v = later(*x1_ev_v.values())

            if STAGE <= 5:
                _stage_exit(x1_ready_v, KH, KL)
                continue

            if not FFN_ON:
                for tb in range(TBN):
                    mo = gdma(lambda g, o=(tok0 + tb * 128) * D, t=tb: g.dma_start(
                        out=bass.AP(out_h, o, [[D, 128], [1, D]]),
                        in_=sslice(X1TM, t * 1024, 1024)),
                        waits=[ln1_x1[tb]], inc="sO")
                    out_ms.append(mo)
                g_A32_free = later(*ln1_x1.values())
                g_XT_free = out_ms[-1]
                g_X1_free = out_ms[-1]
                g_QH_free = z_mm_last
                g_KH_free = None
                g_VH_free = later(*o_done.values())
                g_FU_free = v_mm_last
                continue

            w_ready = {}
            for tb in range(TBN):
                b = tb % 2
                last = None
                for kc in range(KC):
                    wl = ([x1_ready_s, x1_ready_v,
                           rt_ready_s, rt_ready_v] + bankg[b]) if kc == 0 else ()
                    last = mm3(PS[b][:, 0:8],
                               sslice(KH, kc * 512 + tb * 128, 128),
                               sslice(KL, kc * 512 + tb * 128, 128),
                               sslice(RT_HI, kc * 8, 8),
                               sslice(RT_LO, kc * 8, 8),
                               kc == 0, kc == KC - 1,
                               waits=wl, inc=("sT" if kc == KC - 1 else None))
                r = tb * 40
                pm = [None]

                def ch(fn, extra=()):
                    pm[0] = vop(fn, waits=[pm[0]] + list(extra))
                    return pm[0]

                ch(lambda v, bb=b, rr=r: v.reduce_max(RTI[:, rr + 32:rr + 33],
                                                      PS[bb][:, 0:8], axis=AXX),
                   extra=[last])
                mneg = ch(lambda v, rr=r: v.tensor_scalar(RTI[:, rr + 33:rr + 34],
                                                          RTI[:, rr + 32:rr + 33],
                                                          -1.0, None, op0=OP.mult))
                mexp = sop(lambda s, bb=b, rr=r: s.activation(RTI[:, rr:rr + 8],
                                                              PS[bb][:, 0:8], AF.Exp,
                                                              bias=RTI[:, rr + 33:rr + 34]),
                           waits=[mneg])
                bankg[b] = [mexp]
                ch(lambda v, rr=r: v.reduce_sum(RTI[:, rr + 35:rr + 36],
                                                RTI[:, rr:rr + 8], axis=AXX),
                   extra=[mexp])
                ch(lambda v, rr=r: v.reciprocal(RTI[:, rr + 36:rr + 37],
                                                RTI[:, rr + 35:rr + 36]))
                ch(lambda v, rr=r: v.tensor_scalar(RTI[:, rr:rr + 8], RTI[:, rr:rr + 8],
                                                   RTI[:, rr + 36:rr + 37], None,
                                                   op0=OP.mult))
                ch(lambda v, rr=r: v.reduce_max(RTI[:, rr + 32:rr + 33],
                                                RTI[:, rr:rr + 8], axis=AXX))
                mneg1 = ch(lambda v, rr=r: v.tensor_scalar(RTI[:, rr + 33:rr + 34],
                                                           RTI[:, rr + 32:rr + 33],
                                                           -1.0, None, op0=OP.mult))
                ch(lambda v, rr=r: v.tensor_scalar(RTI[:, rr + 8:rr + 16], RTI[:, rr:rr + 8],
                                                   RTI[:, rr + 32:rr + 33], None,
                                                   op0=OP.is_equal))
                ch(lambda v, rr=r: v.reduce_sum(RTI[:, rr + 37:rr + 38],
                                                RTI[:, rr + 8:rr + 16], axis=AXX))
                ch(lambda v, rr=r: v.scalar_tensor_tensor(RTI[:, rr + 16:rr + 24],
                                                          RTI[:, rr + 8:rr + 16],
                                                          -BIG, RTI[:, rr:rr + 8],
                                                          op0=OP.mult, op1=OP.add))
                ch(lambda v, rr=r: v.reduce_max(RTI[:, rr + 38:rr + 39],
                                                RTI[:, rr + 16:rr + 24], axis=AXX))
                ch(lambda v, rr=r: v.tensor_scalar(RTI[:, rr + 37:rr + 38],
                                                   RTI[:, rr + 37:rr + 38],
                                                   2.0, None, op0=OP.is_ge))
                ch(lambda v, rr=r: v.tensor_tensor(RTI[:, rr + 39:rr + 40],
                                                   RTI[:, rr + 32:rr + 33],
                                                   RTI[:, rr + 38:rr + 39],
                                                   op=OP.subtract))
                ch(lambda v, rr=r: v.tensor_tensor(RTI[:, rr + 39:rr + 40],
                                                   RTI[:, rr + 39:rr + 40],
                                                   RTI[:, rr + 37:rr + 38],
                                                   op=OP.mult))
                ch(lambda v, rr=r: v.tensor_tensor(RTI[:, rr + 38:rr + 39],
                                                   RTI[:, rr + 38:rr + 39],
                                                   RTI[:, rr + 39:rr + 40],
                                                   op=OP.add))
                ch(lambda v, rr=r: v.tensor_scalar(RTI[:, rr + 8:rr + 16], RTI[:, rr:rr + 8],
                                                   RTI[:, rr + 38:rr + 39], None,
                                                   op0=OP.is_ge))
                m2m1 = ch(lambda v, rr=r: v.tensor_tensor(RTI[:, rr + 39:rr + 40],
                                                          RTI[:, rr + 38:rr + 39],
                                                          RTI[:, rr + 32:rr + 33],
                                                          op=OP.subtract))
                me2 = sop(lambda s, rr=r: s.activation(RTI[:, rr + 39:rr + 40],
                                                       RTI[:, rr + 39:rr + 40], AF.Exp),
                          waits=[m2m1])
                men = sop(lambda s, rr=r: s.activation(RTI[:, rr + 16:rr + 24],
                                                       RTI[:, rr:rr + 8], AF.Exp,
                                                       bias=RTI[:, rr + 33:rr + 34]),
                          waits=[mneg1])
                ch(lambda v, rr=r: v.tensor_scalar(RTI[:, rr + 39:rr + 40],
                                                   RTI[:, rr + 39:rr + 40],
                                                   1.0, None, op0=OP.add),
                   extra=[me2])
                ch(lambda v, rr=r: v.reciprocal(RTI[:, rr + 36:rr + 37],
                                                RTI[:, rr + 39:rr + 40]))
                ch(lambda v, rr=r: v.tensor_tensor(RTI[:, rr + 16:rr + 24],
                                                   RTI[:, rr + 16:rr + 24],
                                                   RTI[:, rr + 8:rr + 16], op=OP.mult),
                   extra=[men])
                mw = ch(lambda v, rr=r, t=tb: v.tensor_scalar(
                    W32[:, t * 8:t * 8 + 8], RTI[:, rr + 16:rr + 24],
                    RTI[:, rr + 36:rr + 37], None, op0=OP.mult))
                w_ready[tb] = mw

            if STAGE <= 6:
                _stage_exit(later(*w_ready.values()))
                continue

            down_last_half = {}
            fuse_ms = {}
            gu_mm_last = [None]
            ACC = [X1TM, XT32]
            down_prev_md = {}

            def emit_GU(e, hf):
                p0_pump_until(48 * e + 32)
                for fci in range(FH):
                    fc = hf * FH + fci
                    slg, fg = ring_kg.load(
                        ("kg", e, fc),
                        bass.AP(skg_hi, e * 128 * KC * F + fc * 1024, [[KC * F, 128], [1, 1024]]),
                        bass.AP(skg_lo, e * 128 * KC * F + fc * 1024, [[KC * F, 128], [1, 1024]]))
                    slu, fu = ring_ku.load(
                        ("ku", e, fc),
                        bass.AP(sku_hi, e * 128 * KC * F + fc * 1024, [[KC * F, 128], [1, 1024]]),
                        bass.AP(sku_lo, e * 128 * KC * F + fc * 1024, [[KC * F, 128], [1, 1024]]))
                    bg = 4 + (fc % 2)
                    bu = 6 + (fc % 2)
                    lg = lu = None
                    for kc in range(KC):
                        wl = ([fg, x1_ready_s, x1_ready_v] + bankg[bg]) if kc == 0 else ()
                        lg = mm3(PS[bg][:, 0:512],
                                 sslice(KGH, slg * 1024 + kc * 128, 128),
                                 sslice(KGL, slg * 1024 + kc * 128, 128),
                                 sslice(KH, kc * 512, 512), sslice(KL, kc * 512, 512),
                                 kc == 0, kc == KC - 1,
                                 waits=wl, inc=("sT" if kc == KC - 1 else None))
                    ring_kg.consumed(slg, lg)
                    for kc in range(KC):
                        wl = ([fu] + bankg[bu]) if kc == 0 else ()
                        lu = mm3(PS[bu][:, 0:512],
                                 sslice(KUH, slu * 1024 + kc * 128, 128),
                                 sslice(KUL, slu * 1024 + kc * 128, 128),
                                 sslice(KH, kc * 512, 512), sslice(KL, kc * 512, 512),
                                 kc == 0, kc == KC - 1,
                                 waits=wl, inc=("sT" if kc == KC - 1 else None))
                    ring_ku.consumed(slu, lu)
                    gu_mm_last[0] = lu
                    par = fc % 2
                    if e == 0:
                        prev = v_mm_last if hf == 0 else g_FU_free
                    else:
                        prev = down_last_half.get((e - 1, hf))
                    msil = sop(lambda s, p=par, bb=bg: s.activation(
                        sslice(GA32, p * 512, 512), PS[bb][:, 0:512], AF.Silu),
                        waits=[lg, ga_guard[par]])
                    bankg[bg] = [msil]
                    mfu = vop(lambda v, p=par, bb=bu: v.tensor_tensor(
                        sslice(FU32, p * 512, 512), sslice(GA32, p * 512, 512),
                        PS[bb][:, 0:512], op=OP.mult),
                        waits=[msil, lu, fu_guard[par]])
                    bankg[bu] = [mfu]
                    ga_guard[par] = mfu
                    mfh = sop(lambda s, p=par, f=fc: s.copy(
                        sslice(FUH, f * 512, 512), sslice(FU32, p * 512, 512)),
                        waits=[mfu, prev])
                    fu_guard[par] = mfh
                    mfl = vop(lambda v, p=par, f=fc: v.tensor_tensor(
                        sslice(FUL, f * 512, 512), sslice(FU32, p * 512, 512),
                        sslice(FUH, f * 512, 512), op=OP.subtract),
                        waits=[mfh, prev])
                    fuse_ms[(e, fc)] = (mfh, mfl)

            def emit_DOWN(e, hf):
                p0_pump_until(48 * (e + 1))
                grp_last = {}
                last = None
                for fci in range(FH):
                    fc = hf * FH + fci
                    sld, fd = ring_kd.load(
                        ("kd", e, fc),
                        bass.AP(skd_hi, e * 128 * FC * D + fc * 1024, [[FC * D, 128], [1, 1024]]),
                        bass.AP(skd_lo, e * 128 * FC * D + fc * 1024, [[FC * D, 128], [1, 1024]]))
                    for nh in range(2):
                        for tb in range(TBN):
                            b = nh * 4 + tb
                            wl = [fd, fuse_ms[(e, fc)][0], fuse_ms[(e, fc)][1]]
                            if fci == 0:
                                wl = wl + bankg[b]
                            want_inc = (fci == FH - 1) or (nh == 1 and tb == TBN - 1)
                            last = mm3(PS[b][:, 0:512],
                                       sslice(FUH, fc * 512 + tb * 128, 128),
                                       sslice(FUL, fc * 512 + tb * 128, 128),
                                       sslice(KDH, sld * 1024 + nh * 512, 512),
                                       sslice(KDL, sld * 1024 + nh * 512, 512),
                                       fci == 0, fci == FH - 1,
                                       waits=wl,
                                       inc=("sT" if want_inc else None))
                            if fci == FH - 1:
                                grp_last[(nh, tb)] = last
                    ring_kd.consumed(sld, last)
                down_last_half[(e, hf)] = later(*grp_last.values())
                ridx = e * 2 + hf
                rbuf = ACC[ridx % 2]
                wbuf = ACC[(ridx + 1) % 2]
                for nh in range(2):
                    for tb in range(TBN):
                        b = nh * 4 + tb
                        md = vop(lambda v, n=nh, t=tb, bb=b, ee=e, rb=rbuf, wb=wbuf:
                                 v.scalar_tensor_tensor(
                                     sslice(wb, t * 1024 + n * 512, 512),
                                     PS[bb][:, 0:512], W32[:, t * 8 + ee:t * 8 + ee + 1],
                                     sslice(rb, t * 1024 + n * 512, 512),
                                     op0=OP.mult, op1=OP.add),
                                 waits=[grp_last[(nh, tb)], w_ready[tb],
                                        down_prev_md.get((nh, tb))])
                        down_prev_md[(nh, tb)] = md
                        bankg[b] = [md]

            if STAGE <= 6.1:
                mset = [scratch_ready[("kg", 0, fc)] for fc in range(FC)]
                mo = em.op("gpsimd", lambda g: g.dma_start(
                    out=bass.AP(dbh_h, 0, [[8192, 128], [1, 8192]]),
                    in_=bass.AP(skg_hi, 0, [[KC * F, 128], [1, 8192]])),
                    waits=[later(*mset)], inc="sO", inc_n=16)
                out_ms.append(mo)
                mo = em.op("gpsimd", lambda g: g.dma_start(
                    out=bass.AP(dbl_h, 0, [[8192, 128], [1, 8192]]),
                    in_=bass.AP(skg_lo, 0, [[KC * F, 128], [1, 8192]])),
                    waits=[], inc="sO", inc_n=16)
                out_ms.append(mo)
                _stage_exit(x1_ready_v)
                continue
            if STAGE <= 6.2:
                emit_GU(0, 0)
                _stage_exit(fuse_ms[(0, FH - 1)][1], FUH, FUL)
                continue
            if STAGE <= 6.3:
                emit_GU(0, 0)
                emit_GU(0, 1)
                emit_DOWN(0, 0)
                emit_DOWN(0, 1)
                _stage_exit(later(*[down_prev_md[k] for k in down_prev_md]), f32_sb=X1TM)
                continue
            emit_GU(0, 0)
            emit_GU(0, 1)
            for e in range(E):
                emit_DOWN(e, 0)
                if e + 1 < E:
                    emit_GU(e + 1, 0)
                emit_DOWN(e, 1)
                if e + 1 < E:
                    emit_GU(e + 1, 1)

            if STAGE <= 6.5:
                _stage_exit(later(*[bankg[i][0] for i in range(8)]), FUH, FUL, f32_sb=X1TM)
                continue

            down_all = later(*[bankg[i][0] for i in range(8)])
            ln2_done = []
            x2_ms = {}
            for tb in range(TBN):
                c = tb * 1024
                m0 = vop(lambda v, cc=c, t=tb: v.reduce_sum(STT[:, 32 + t:33 + t],
                                                            sslice(X1TM, cc, 1024), axis=AXX),
                         waits=[down_all])
                m1 = vop(lambda v, t=tb: v.tensor_scalar(STT[:, 32 + t:33 + t],
                                                         STT[:, 32 + t:33 + t],
                                                         1.0 / D, None, op0=OP.mult),
                         waits=[m0])
                mhm2 = vop(lambda v, cc=c, t=tb: v.tensor_scalar(
                    sslice(XT32, cc, 1024), sslice(X1TM, cc, 1024),
                    STT[:, 32 + t:33 + t], None, op0=OP.subtract), waits=[m1])
                msqa2 = sop(lambda s, cc=c, t=tb: s.activation(
                    sslice(A32, cc, 1024), sslice(XT32, cc, 1024), AF.Square),
                    waits=[mhm2])
                ln2_done.append(msqa2)
                m2 = vop(lambda v, cc=c, t=tb: v.reduce_sum(STT[:, 40 + t:41 + t],
                                                            sslice(A32, cc, 1024), axis=AXX),
                         waits=[msqa2])
                mv = vop(lambda v, t=tb: v.tensor_scalar(STT[:, 40 + t:41 + t],
                                                         STT[:, 40 + t:41 + t],
                                                         1.0 / D, None, op0=OP.mult),
                         waits=[m2])
                msq = sop(lambda s, t=tb: s.activation(STT[:, 48 + t:49 + t],
                                                       STT[:, 40 + t:41 + t],
                                                       AF.Sqrt, bias=LDC[:, 3:4]),
                          waits=[mv, c_eps])
                mr = vop(lambda v, t=tb: v.reciprocal(STT[:, 56 + t:57 + t],
                                                      STT[:, 48 + t:49 + t]),
                         waits=[msq])
                mx2 = vop(lambda v, cc=c, t=tb: v.tensor_scalar(
                    sslice(XT32, cc, 1024), sslice(XT32, cc, 1024),
                    STT[:, 56 + t:57 + t], None, op0=OP.mult), waits=[mr])
                x2_ms[tb] = mx2
            if STAGE <= 6.7:
                _stage_exit(later(*x2_ms.values()), f32_sb=XT32)
                continue

            x2_all = later(*x2_ms.values())
            for tb in range(TBN):
                mo = gdma(lambda g, o=(tok0 + tb * 128) * D, t=tb: g.dma_start(
                    out=bass.AP(out_h, o, [[D, 128], [1, D]]),
                    in_=sslice(XT32, t * 1024, 1024)),
                    waits=[x2_all], inc="sO")
                out_ms.append(mo)

            g_A32_free = later(*ln2_done)
            g_XT_free = out_ms[-1]
            g_X1_free = None
            g_QH_free = z_mm_last
            g_KH_free = gu_mm_last[0]
            g_VH_free = later(*o_done.values())
            g_FU_free = down_last_half[(E - 1, 1)]

        em.op("gpsimd", lambda g: g.nop(), waits=[("sO", em.counts["sO"])])

        def runner(engname):
            prog = em.progs[engname]

            def go(eng):
                for (waits, fn, mark, n) in prog:
                    for (s, v) in waits:
                        eng.wait_ge(semh[s], v)
                    instr = fn(eng)
                    if mark is not None:
                        instr.then_inc(semh[mark[0]], n)
            return go

        block.sync(runner("sync"))
        block.gpsimd(runner("gpsimd"))
        block.scalar(runner("scalar"))
        block.vector(runner("vector"))
        block.tensor(runner("tensor"))

    return nc


def _host_consts():
    p = np.arange(128)
    x = np.arange(128)
    dmat = (x[None, :] % 32 - p[:, None] % 32).astype(np.float32)
    bmat = ((x[None, :] // 32 == p[:, None] // 32) &
            (x[None, :] % 32 >= p[:, None] % 32)).astype(np.float32)
    return np.eye(128, dtype=np.float32), dmat, bmat


def _lay_attn(W):
    w4 = np.asarray(W, np.float32).reshape(KC, 128, KC, 128)
    return np.ascontiguousarray(w4.transpose(3, 0, 2, 1).reshape(128, KC * D))


def _lay_attn_v(W):
    wt = np.asarray(W, np.float32).T.reshape(KC, 128, D)
    return np.ascontiguousarray(wt.transpose(1, 0, 2).reshape(128, KC * D))


def kernel(L0_attn_x, L0_attn_W_q, L0_attn_W_k, L0_attn_W_v, L0_attn_W_g,
           L0_attn_W_o, L0_attn_log_decay, L0_ffn_router_kernel_DE,
           L0_ffn_kernel_gating_EDF, L0_ffn_kernel_up_EDF, L0_ffn_kernel_down_EFD):
    x = np.ascontiguousarray(np.asarray(L0_attn_x, dtype=np.float32)).reshape(NCORE * TOK, D)
    ident, dmat, bmat = _host_consts()
    kg = np.asarray(L0_ffn_kernel_gating_EDF, np.float32).reshape(E, KC, 128, FC, 128)
    kgp = np.ascontiguousarray(kg.transpose(0, 2, 3, 1, 4).reshape(E, 128, KC * F))
    ku = np.asarray(L0_ffn_kernel_up_EDF, np.float32).reshape(E, KC, 128, FC, 128)
    kup = np.ascontiguousarray(ku.transpose(0, 2, 3, 1, 4).reshape(E, 128, KC * F))
    kd = np.asarray(L0_ffn_kernel_down_EFD, np.float32).reshape(E, FC, 128, D)
    kdp = np.ascontiguousarray(kd.transpose(0, 2, 1, 3).reshape(E, 128, FC * D))
    rt = np.asarray(L0_ffn_router_kernel_DE, np.float32).reshape(KC, 128, E)
    rtp = np.ascontiguousarray(rt.transpose(1, 0, 2).reshape(128, KC * E))
    shared = {
        "wqT": _lay_attn(L0_attn_W_q),
        "wkT": _lay_attn(L0_attn_W_k),
        "wgT": _lay_attn(L0_attn_W_g),
        "wvT": _lay_attn_v(L0_attn_W_v),
        "woT": _lay_attn(L0_attn_W_o),
        "router": rtp,
        "kg": kgp,
        "ku": kup,
        "kd": kdp,
        "log_decay": np.ascontiguousarray(np.asarray(L0_attn_log_decay, np.float32)),
        "ident": ident, "dmat": dmat, "bmat": bmat,
    }
    in_maps = []
    for c in range(NCORE):
        xs = x[c * TOK:(c + 1) * TOK]
        m = dict(shared)
        m["x_tm"] = np.ascontiguousarray(xs)
        m["x_fm"] = np.ascontiguousarray(xs.T)
        in_maps.append(m)
    nc = build()
    res = run_bass_kernel_spmd(nc, in_maps, core_ids=list(range(NCORE)))
    outs = [np.asarray(res.results[c]["out"]) for c in range(NCORE)]
    return np.concatenate(outs, axis=0).reshape(2, 8192, 1024)
